# version 02
# baseline (speedup 1.0000x reference)
import contextlib
import numpy as np
import ml_dtypes
import concourse.bass as bass
import concourse.mybir as mybir
from concourse.bass_utils import run_bass_kernel_spmd

F32 = mybir.dt.float32
BF16 = mybir.dt.bfloat16
I32 = mybir.dt.int32
AF = mybir.ActivationFunctionType
ALU = mybir.AluOpType

NCORE = 8
SEQ = 8192
TOK = SEQ // NCORE
D = 2048
NCH = D // 128
TT = 512
NTT = TOK // TT
ENGINES = ("tensor", "vector", "scalar", "gpsimd", "sync")

D_INNER = 4096
D_XB = 512
N_HEADS = 64
D_STATE = 64
CHUNK = 64
N_CHUNK = TOK // CHUNK
INTER = 5632
CONV_DIM = 2 * D_XB + D_INNER
ZXD = D_INNER + CONV_DIM + N_HEADS


class P:

    def __init__(self, nc, es):
        self.nc = nc
        self.es = es
        self.ops = {e: [] for e in ENGINES}
        self.cnt = {}
        self.semh = {}

    def sem(self, name):
        self.semh[name] = self.es.enter_context(self.nc.semaphore(name))
        self.cnt[name] = 0
        return name

    def on(self, eng, fn):
        self.ops[eng].append(fn)

    def wait(self, eng, name, val=None):
        if val is None:
            val = self.cnt[name]
        assert val <= self.cnt[name], ("wait-ahead!", eng, name, val, self.cnt[name])
        if val <= 0:
            return
        h = self.semh[name]
        self.on(eng, lambda e, h=h, v=val: e.wait_ge(h, v))

    def op(self, eng, f, sem=None, amt=1):
        if sem is None:
            self.on(eng, f)
            return None
        self.cnt[sem] += amt
        h = self.semh[sem]
        self.on(eng, lambda e, f=f, h=h, a=amt: f(e).then_inc(h, a))
        return self.cnt[sem]

    def dma(self, q, out, in_, sem=None, slow=False):
        return self.op(
            q, lambda e: e.dma_start(out=out, in_=in_,
                                     allow_slow_non_contiguous=slow), sem, 16)

    def finalize(self, block):
        for eng in ENGINES:
            ops = self.ops[eng]
            if not ops:
                continue

            def runner(e, ops=ops):
                for f in ops:
                    f(e)

            getattr(block, eng)(runner)


def host_constants(core):
    c = {}
    theta = (10000.0 ** (-(np.arange(64, dtype=np.float32) * 2.0) / 128.0)).astype(np.float32)
    pos = np.arange(SEQ, dtype=np.float32)
    ang = pos[None, :] * theta[:, None]
    c["c_cos"] = np.cos(ang).astype(np.float32)
    c["c_sin"] = np.sin(ang).astype(np.float32)
    c["c_eye128"] = np.eye(128, dtype=np.float32)
    tl = np.tril(np.ones((64, 64), np.float32)).T
    c["c_tril64"] = np.concatenate([tl, tl], axis=0).astype(ml_dtypes.bfloat16)
    c["c_eye128b"] = np.eye(128, dtype=ml_dtypes.bfloat16)
    ki = np.arange(128)[:, None]
    qi = np.arange(896)[None, :] - 384
    c["c_cmask"] = (qi >= ki).astype(np.float32).astype(ml_dtypes.bfloat16)
    rm = (np.arange(NCORE) < core).astype(np.float32)
    pm = (np.arange(NCORE) == core - 1).astype(np.float32)
    c["c_rmask"] = np.tile(rm[None, :], (128, 1)).astype(np.float32)
    c["c_rinv"] = (1.0 - c["c_rmask"]).astype(np.float32)
    c["c_pmask"] = np.tile(pm[None, :], (128, 1)).astype(np.float32)
    return c


SHARED_SPACE = "Shared"

class MK:
    def __init__(self):
        self.nc = bass.Bass("TRN2", target_bir_lowering=False, debug=False)
        nc = self.nc
        ein = lambda n, s, d: nc.dram_tensor(n, s, d, kind="ExternalInput").ap()
        self.xin = ein("xin", [TOK, D], F32)
        self.wq_h = ein("wq_h", [256, D], F32)
        self.wk_h = ein("wk_h", [256, D], F32)
        self.wv_h = ein("wv_h", [256, D], F32)
        self.wo = ein("wo", [D, D], F32)
        self.g0w = ein("g0w", [8192, D], I32)
        self.u0w = ein("u0w", [8192, D], I32)
        self.d0w = ein("d0w", [D, 8192], I32)
        self.g0s = ein("g0s", [8192], F32)
        self.u0s = ein("u0s", [8192], F32)
        self.d0s = ein("d0s", [D], F32)
        self.w_in = ein("w_in", [D, ZXD], F32)
        self.conv_w = ein("conv_w", [CONV_DIM, 4], F32)
        self.conv_b = ein("conv_b", [CONV_DIM], F32)
        self.dt_bias = ein("dt_bias", [N_HEADS], F32)
        self.a_log = ein("a_log", [N_HEADS], F32)
        self.d_vec = ein("d_vec", [N_HEADS], F32)
        self.norm_w = ein("norm_w", [D_INNER], F32)
        self.w_out = ein("w_out", [D_INNER, D], F32)
        self.ln1_w = ein("ln1_w", [D], F32)
        self.ln2_w = ein("ln2_w", [D], F32)
        self.mgw = ein("mgw", [D, INTER], F32)
        self.muw = ein("muw", [D, INTER], F32)
        self.mdw = ein("mdw", [INTER, D], F32)
        self.f1w0 = ein("f1w0", [8192, D], F32)
        self.f1b0 = ein("f1b0", [8192], F32)
        self.f1w1 = ein("f1w1", [D, 8192], F32)
        self.f1b1 = ein("f1b1", [D], F32)
        self.c_cos = ein("c_cos", [64, SEQ], F32)
        self.c_sin = ein("c_sin", [64, SEQ], F32)
        self.c_eye128 = ein("c_eye128", [128, 128], F32)
        self.c_tril64 = ein("c_tril64", [128, 64], BF16)
        self.c_eye128b = ein("c_eye128b", [128, 128], BF16)
        self.c_cmask = ein("c_cmask", [128, 896], BF16)
        self.c_rmask = ein("c_rmask", [128, 8], F32)
        self.c_rinv = ein("c_rinv", [128, 8], F32)
        self.c_pmask = ein("c_pmask", [128, 8], F32)

        self.yout = nc.dram_tensor("yout", [TOK, D], F32, kind="ExternalOutput").ap()

        dr = lambda n, s, d, **kw: nc.dram_tensor(n, s, d, kind="Internal", **kw).ap()
        self.st_g0 = dr("st_g0", [8192, D], BF16)
        self.st_u0 = dr("st_u0", [8192, D], BF16)
        self.st_d0 = dr("st_d0", [D, 8192], BF16)
        self.st_f0 = dr("st_f0", [8192, D], BF16)
        self.st_f1 = dr("st_f1", [D, 8192], BF16)
        self.st_qkv = dr("st_qkv", [768, D], BF16)
        self.st_wo = dr("st_wo", [D, D], BF16)
        self.g0T = dr("g0T", [D, 8192], BF16)
        self.u0T = dr("u0T", [D, 8192], BF16)
        self.d0T = dr("d0T", [8192, D], BF16)
        self.f1w0T = dr("f1w0T", [D, 8192], BF16)
        self.f1w1T = dr("f1w1T", [8192, D], BF16)
        self.qkvT = dr("qkvT", [D, 768], BF16)
        self.woT = dr("woT", [D, D], BF16)
        self.vsc = dr("vsc", [SEQ, 256], BF16)
        self.gb_xln_in = dr("gb_xln_in", [D, TOK], BF16)
        self.gb_xln_out = dr("gb_xln_out", [NCORE * D, TOK], BF16, addr_space=SHARED_SPACE)
        self.gb_ao_in = dr("gb_ao_in", [D, TOK], BF16)
        self.gb_ao_out = dr("gb_ao_out", [D, TOK], BF16)
        self.gb_halo_in = dr("gb_halo_in", [128, 160], BF16)
        self.gb_halo_out = dr("gb_halo_out", [NCORE * 128, 160], BF16, addr_space=SHARED_SPACE)
        self.gb_seg_in = dr("gb_seg_in", [128, 4096], F32)
        self.gb_seg_out = dr("gb_seg_out", [NCORE * 128, 4096], F32, addr_space=SHARED_SPACE)
        self.m_z = dr("m_z", [D_INNER, TOK], BF16)
        self.m_xbc = dr("m_xbc", [CONV_DIM, TOK], BF16)
        self.m_cod = dr("m_cod", [128, 32768], BF16)
        self.m_ydiag = dr("m_ydiag", [D_INNER, TOK], BF16)
        self.m_rows = dr("m_rows", [64, TOK], F32)
        self.m_rows2 = dr("m_rows2", [TOK, 64], F32)
        self.m_rows3 = dr("m_rows3", [64, 32], F32)
        self.m_cT = dr("m_cT", [D_INNER, TOK], BF16)
        self.m_rows2w = dr("m_rows2w", [TOK, 64], F32)
        self.m_lnx = dr("m_lnx", [D, TOK], BF16)
        self.m_acum = dr("m_acum", [64, TOK], F32)
        self.m_od = dr("m_od", [64, TOK], F32)
        self.m_yd2 = dr("m_yd2", [128, 32768], BF16)
        self.m_prev = dr("m_prev", [N_CHUNK * 128, 2048], F32)
        self.m_dbg = dr("m_dbg", [64, 4096], F32)
        self.m_mo = dr("m_mo", [D, TOK], BF16)

        es = self.es = contextlib.ExitStack()
        es.__enter__()
        self.p = P(nc, es)
        self._alloc()

    def _alloc(self):
        nc, es, p = self.nc, self.es, self.p
        sb = lambda n, s, d: es.enter_context(nc.sbuf_tensor(n, s, d))
        self.xres = sb("xres", [128, NCH * TOK], F32)
        self.xln = sb("xln", [128, NCH * TT], BF16)
        self.hbuf = sb("hbuf", [128, 32768], BF16)
        self.wt = sb("wt", [128, 13056], BF16)
        self.scr = sb("scr", [128, 3072], F32)
        self.scrb = sb("scrb", [128, 2176], BF16)
        self.prs = sb("prs", [128, 2816], BF16)
        self.ones_col = sb("ones_col", [128, 1], BF16)
        self.ones_row = sb("ones_row", [1, 512], BF16)
        self.ones_r128b = sb("ones_r128b", [1, 128], BF16)
        self.ones_r128f = sb("ones_r128f", [1, 128], F32)
        self.mrow = sb("mrow", [1, 128], BF16)
        self.stat = sb("stat", [1, 3 * TT], F32)
        self.statb = sb("statb", [1, 2 * TT // 2], BF16)
        self.sc_g = sb("sc_g", [128, 64], F32)
        self.sc_u = sb("sc_u", [128, 64], F32)
        self.sc_d = sb("sc_d", [128, 16], F32)
        self.b0pp = sb("b0pp", [128, 64], F32)
        self.konst = sb("konst", [128, 8], F32)
        self.cbpp = sb("cbpp", [128, 40], F32)
        self.b1row = sb("b1row", [1, D], BF16)
        self.g_ln = sb("g_ln", [128, 2 * NCH], F32)
        self.g_nw = sb("g_nw", [128, 36], F32)
        self.eye128 = sb("eye128", [128, 128], F32)
        self.tril64 = sb("tril64", [128, 64], BF16)
        self.eye128b = sb("eye128b", [128, 128], BF16)
        self.rmask = sb("rmask", [128, 8], F32)
        self.rinv = sb("rinv", [128, 8], F32)
        self.pmask = sb("pmask", [128, 8], F32)
        self.psum = [es.enter_context(nc.psum_tensor(f"ps{i}", [128, 512], F32))
                     for i in range(8)]
        for n in ["lw0", "lw1", "lw2", "lw3",
                  "lg0", "lg1",
                  "lp0", "lp1",
                  "sp0", "sp1",
                  "lx", "lsm", "lgs",
                  "st0", "st1", "st2", "stg",
                  "cc",
                  "ts", "vs", "ss", "tp", "tq"]:
            p.sem(n)

    def mm(self, out, lhsT, rhs, start, stop, sem=None):
        return self.p.op(
            "tensor",
            lambda e: e.matmul(out, lhsT, rhs, start=start, stop=stop,
                               skip_group_check=True),
            sem)

    def emit_smalls(self):
        p = self.p
        for dst, src in [(self.eye128[:, :], self.c_eye128),
                         (self.tril64[:, :], self.c_tril64),
                         (self.eye128b[:, :], self.c_eye128b),
                         (self.rmask[:, :], self.c_rmask),
                         (self.rinv[:, :], self.c_rinv),
                         (self.pmask[:, :], self.c_pmask)]:
            p.dma("sync", dst, src, "lsm")

        def vec_pp(dst, src):
            p.dma("sync", dst, src.rearrange("(c p) -> p c", p=128), "lsm", slow=True)

        vec_pp(self.sc_g[:, 0:64], self.g0s)
        vec_pp(self.sc_u[:, 0:64], self.u0s)
        vec_pp(self.sc_d[:, 0:16], self.d0s)
        vec_pp(self.g_ln[:, 0:NCH], self.ln1_w)
        vec_pp(self.g_ln[:, NCH:2 * NCH], self.ln2_w)
        vec_pp(self.g_nw[:, 0:32], self.norm_w)
        vec_pp(self.b0pp[:, 0:64], self.f1b0)
        p.dma("sync", self.cbpp[:, 0:40],
              self.conv_b.rearrange("(c p) -> p c", p=128), "lsm", slow=True)
        p.dma("sync", self.g_nw[0:64, 32:33], self.dt_bias.unsqueeze(1), "lsm", slow=True)
        p.dma("sync", self.g_nw[0:64, 33:34], self.a_log.unsqueeze(1), "lsm", slow=True)
        p.dma("gpsimd", self.b1row[0:1, :], self.f1b1.unsqueeze(0), "lgs")
        p.op("vector", lambda e: e.memset(self.ones_col[:, :], 1.0), "vs")
        p.op("vector", lambda e: e.memset(self.ones_row[:, :], 1.0), "vs")
        p.op("vector", lambda e: e.memset(self.ones_r128b[:, :], 1.0), "vs")
        p.op("vector", lambda e: e.memset(self.ones_r128f[:, :], 1.0), "vs")
        p.op("vector", lambda e: e.memset(self.mrow[:, :], -128.0), "vs")
        p.op("vector", lambda e: e.memset(self.konst[:, 0:1], 1e-5), "vs")
        p.op("vector", lambda e: e.memset(self.konst[:, 1:2], -0.5), "vs")
        p.op("vector", lambda e: e.memset(self.konst[:, 2:3], 1e-6), "vs")
        p.op("vector", lambda e: e.memset(self.konst[:, 3:4], -1.0), "vs")
        p.op("vector", lambda e: e.memset(self.konst[:, 4:5], 1.0), "vs")
        p.wait("scalar", "vs")
        p.wait("vector", "lsm")
        p.op("vector", lambda e: e.tensor_scalar_mul(
            self.sc_u[:, 0:64], self.sc_u[:, 0:64], 0.5), "vs")
        p.wait("vector", "lgs")
        p.wait("scalar", "lsm")

    def emit_input(self):
        p = self.p
        for tt in range(TOK // 128):
            buf = self.scr[:, 0:2048]
            if tt >= 1:
                p.wait("sync", "tp")
            p.dma("sync", buf, self.xin[tt * 128:(tt + 1) * 128, :], "lx")
            p.wait("tensor", "lx")
            for c in range(NCH):
                pb = self.psum[c % 2][:, 0:128]
                p.wait("tensor", "vs")
                p.op("tensor", lambda e, pb=pb, buf=buf, c=c: e.transpose(
                    pb, buf[:, c * 128:(c + 1) * 128], self.eye128[:, :]), "tp")
                p.wait("vector", "tp")
                dst = self.xres[:, c * TOK + tt * 128: c * TOK + (tt + 1) * 128]
                p.op("vector", lambda e, dst=dst, pb=pb: e.tensor_copy(dst, pb), "vs")

    def emit_output(self):
        p = self.p
        for e in ("tensor", "vector", "sync"):
            for s in ("ts", "vs", "ss", "tp", "tq"):
                p.wait(e, s)
        for tt in range(TOK // 128):
            buf = self.scr[:, 0:2048]
            stsem = "st0"
            if tt >= 1:
                p.wait("vector", stsem)
            for c in range(NCH):
                pb = self.psum[c % 2][:, 0:128]
                p.wait("tensor", "vs")
                src = self.xres[:, c * TOK + tt * 128: c * TOK + (tt + 1) * 128]
                p.op("tensor", lambda e, pb=pb, src=src: e.transpose(
                    pb, src, self.eye128[:, :]), "tp")
                p.wait("vector", "tp")
                p.op("vector", lambda e, buf=buf, c=c, pb=pb: e.tensor_copy(
                    buf[:, c * 128:(c + 1) * 128], pb), "vs")
            p.wait("sync", "vs")
            p.dma("sync", self.yout[tt * 128:(tt + 1) * 128, :], buf, stsem)
        p.wait("sync", "st0")
        p.wait("sync", "st1")

    _cast_rot = 0

    def emit_cast(self, src, dst, rows, cols):
        p = self.p
        sem = "lg0" if MK._cast_rot % 2 == 0 else "lg1"
        MK._cast_rot += 1
        p.wait("gpsimd", sem)
        RB = 4096
        for r0 in range(0, rows, RB):
            r1 = min(rows, r0 + RB)
            p.dma("gpsimd", dst[r0:r1, 0:cols], src[r0:r1, :], sem)
        return sem, p.cnt[sem]

    def emit_transpose_prep(self, stg, dstT, rows, cols, cast_done):
        p = self.p
        if isinstance(cast_done, tuple):
            cast_done = [cast_done]
        for s, v in cast_done:
            p.wait("scalar", s, v)
        RB = 1408
        tiles = [(b, r0) for b in range(cols // 128) for r0 in range(0, rows, RB)]
        n = len(tiles)
        lcnt = {}
        for idx in range(n + 1):
            if idx < n:
                b, r0 = tiles[idx]
                par = idx % 2
                half = self.prs[:, par * 1408:(par + 1) * 1408]
                lsem = "lp0" if par == 0 else "lp1"
                rl = min(RB, rows - r0)
                if idx >= 2:
                    p.wait("scalar", "sp0" if par == 0 else "sp1")
                p.op("scalar", lambda e, half=half, b=b, r0=r0, rl=rl, stg=stg: e.dma_start(
                    out=half[:, 0:rl], in_=stg[r0:r0 + rl, b * 128:(b + 1) * 128],
                    transpose=True), lsem, 16)
                lcnt[idx] = p.cnt[lsem]
            if idx >= 1:
                j = idx - 1
                b, r0 = tiles[j]
                par = j % 2
                half = self.prs[:, par * 1408:(par + 1) * 1408]
                rl = min(RB, rows - r0)
                p.wait("scalar", "lp0" if par == 0 else "lp1", lcnt[j])
                p.op("scalar", lambda e, half=half, b=b, r0=r0, rl=rl, dstT=dstT: e.dma_start(
                    out=dstT[b * 128:(b + 1) * 128, r0:r0 + rl], in_=half[:, 0:rl]),
                    "sp0" if par == 0 else "sp1", 16)
        p.wait("scalar", "sp0")
        p.wait("scalar", "sp1")
        return p.cnt["sp0"], p.cnt["sp1"]

    def emit_prep(self, src, stg, dstT, rows, cols):
        cd = self.emit_cast(src, stg, rows, cols)
        return self.emit_transpose_prep(stg, dstT, rows, cols, cd)

    def emit_layernorm(self, T, rms=False, gain_col=None, eps=1e-5):
        p = self.p
        ps_s = self.psum[6][0:1, 0:512]
        ps_q = self.psum[7][0:1, 0:512]
        ps_a = self.psum[6][:, 0:512]
        ps_b = self.psum[7][:, 0:512]
        p.wait("scalar", "vs")
        ln_gate = {}
        for c in range(NCH):
            par = c % 2
            src = self.xres[:, c * TOK + T * TT: c * TOK + T * TT + TT]
            xbf = self.scrb[:, par * 1024: par * 1024 + 512]
            xsq = self.scrb[:, par * 1024 + 512: par * 1024 + 1024]
            if c >= 2:
                p.wait("scalar", "ts", ln_gate[c - 2])
            if not rms:
                p.op("scalar", lambda e, xbf=xbf, src=src: e.activation(xbf, src, AF.Copy), "ss")
            p.op("scalar", lambda e, xsq=xsq, src=src: e.activation(xsq, src, AF.Square), "ss")
            p.wait("tensor", "ss")
            if not rms:
                self.mm(ps_s, self.ones_col[:, :], xbf, c == 0, c == NCH - 1)
            self.mm(ps_q, self.ones_col[:, :], xsq, c == 0, c == NCH - 1, "ts")
            ln_gate[c] = p.cnt["ts"]
        p.wait("vector", "ts")
        s_m = self.stat[0:1, 0:512]
        s_v = self.stat[0:1, 512:1024]
        s_b = self.stat[0:1, 1024:1536]
        s_t = s_b
        if not rms:
            p.op("vector", lambda e: e.tensor_scalar_mul(s_m, ps_s, 1.0 / D), "vs")
        p.op("vector", lambda e: e.tensor_scalar_mul(s_v, ps_q, 1.0 / D), "vs")
        if not rms:
            p.wait("vector", "vs")
            p.op("vector", lambda e: e.tensor_mul(s_t, s_m, s_m), "vs")
            p.wait("vector", "vs")
            p.op("vector", lambda e: e.tensor_sub(s_v, s_v, s_t), "vs")
        p.wait("scalar", "vs")
        ek = self.konst[0:1, 0:1] if eps == 1e-5 else self.konst[0:1, 2:3]
        p.op("scalar", lambda e, ek=ek: e.activation(s_v, s_v, AF.Ln, bias=ek), "ss")
        p.wait("scalar", "ss")
        p.op("scalar", lambda e: e.activation(s_v, s_v, AF.Exp, scale=self.konst[0:1, 1:2]), "ss")
        p.wait("vector", "ss")
        if not rms:
            p.op("vector", lambda e: e.tensor_mul(s_b, s_m, s_v), "vs")
        p.wait("tensor", "vs")
        p.wait("tensor", "ss")
        self.mm(ps_a, self.ones_r128f[:, :], s_v, True, True, "tq")
        if not rms:
            self.mm(ps_b, self.ones_r128f[:, :], s_b, True, True, "tq")
        p.wait("vector", "tq")
        for c in range(NCH):
            src = self.xres[:, c * TOK + T * TT: c * TOK + T * TT + TT]
            dst = self.xln[:, c * TT:(c + 1) * TT]
            if rms:
                g = self.g_ln[:, gain_col + c:gain_col + c + 1] if gain_col is not None else None
                p.op("vector", lambda e, dst=dst, src=src, g=g: e.scalar_tensor_tensor(
                    dst, src, g, ps_a, ALU.mult, ALU.mult), "vs")
            else:
                tmp = self.scr[:, 2048:2560]
                if c > 0:
                    p.wait("vector", "vs")
                p.op("vector", lambda e, tmp=tmp, src=src: e.tensor_mul(tmp, src, ps_a), "vs")
                p.wait("vector", "vs")
                p.op("vector", lambda e, dst=dst, tmp=tmp: e.tensor_sub(dst, tmp, ps_b), "vs")
        p.wait("tensor", "vs")

    def emit_ffn_quant(self):
        p = self.p
        for T in range(NTT):
            self.emit_layernorm(T)
            for s in ("lw0", "lw1", "lw2", "lw3"):
                self.p.wait("sync", s)
            ps_h = self.psum[4][0:1, 0:512]
            ss_gate = {}
            vs_gate = {}
            for i in range(64):
                par = i % 2
                wg = self.wt[:, par * 2048:(par + 1) * 2048]
                wu = self.wt[:, 4096 + par * 2048: 4096 + (par + 1) * 2048]
                lsemg = "lw0" if par == 0 else "lw1"
                lsemu = "lw2" if par == 0 else "lw3"
                if i >= 2:
                    p.wait("sync", "ss", ss_gate[i - 2])
                    p.wait("sync", "vs", vs_gate[i - 2])
                p.dma("sync", wg.rearrange("p (k o) -> p k o", o=128),
                      self.g0T[:, i * 128:(i + 1) * 128].rearrange("(k p) o -> p k o", p=128),
                      lsemg)
                p.dma("sync", wu.rearrange("p (k o) -> p k o", o=128),
                      self.u0T[:, i * 128:(i + 1) * 128].rearrange("(k p) o -> p k o", p=128),
                      lsemu)
                ps_g = self.psum[par][:, 0:512]
                ps_u = self.psum[2 + par][:, 0:512]
                p.wait("tensor", lsemg)
                p.wait("tensor", lsemu)
                if i >= 2:
                    p.wait("tensor", "ss", ss_gate[i - 2])
                    p.wait("tensor", "vs", vs_gate[i - 2])
                for k in range(NCH):
                    self.mm(ps_g, wg[:, k * 128:(k + 1) * 128],
                            self.xln[:, k * TT:(k + 1) * TT], k == 0, k == NCH - 1)
                for k in range(NCH):
                    self.mm(ps_u, wu[:, k * 128:(k + 1) * 128],
                            self.xln[:, k * TT:(k + 1) * TT], k == 0, k == NCH - 1,
                            "ts" if k == NCH - 1 else None)
                ht = self.hbuf[:, i * 512:(i + 1) * 512]
                sg = self.scrb[:, par * 512:(par + 1) * 512]
                p.wait("scalar", "ts")
                p.op("scalar", lambda e, sg=sg, ps_g=ps_g, i=i: e.activation(
                    sg, ps_g, AF.Silu, scale=self.sc_g[:, i:i + 1]), "ss")
                ss_gate[i] = p.cnt["ss"]
                p.wait("vector", "ss")
                p.op("vector", lambda e, ht=ht, ps_u=ps_u, sg=sg, i=i: e.scalar_tensor_tensor(
                    ht, ps_u, self.sc_u[:, i:i + 1], sg, ALU.mult, ALU.mult), "vs")
                vs_gate[i] = p.cnt["vs"]
                p.wait("tensor", "vs")
                self.mm(ps_h, self.ones_col[:, :], ht, i == 0, i == 63,
                        "ts" if i == 63 else None)
            p.wait("vector", "ts")
            hs = self.statb[0:1, 0:512]
            p.op("vector", lambda e, hs=hs: e.tensor_copy(hs, ps_h), "vs")
            p.wait("tensor", "vs")
            for s in ("lw0", "lw1"):
                p.wait("sync", s)
            for fg in range(4):
                ts_gate = {}
                for s in ("lw0", "lw1"):
                    p.wait("sync", s)
                p.wait("sync", "ts")
                for i in range(64):
                    par = i % 2
                    wd = self.wt[:, 8192 + par * 512: 8192 + (par + 1) * 512]
                    lsem = "lw0" if par == 0 else "lw1"
                    if i >= 2:
                        p.wait("sync", "ts", ts_gate[i - 2])
                    p.dma("sync", wd,
                          self.d0T[i * 128:(i + 1) * 128, fg * 512:(fg + 1) * 512],
                          lsem)
                    p.wait("tensor", lsem)
                    if i == 0:
                        p.wait("tensor", "vs")
                    for j in range(4):
                        self.mm(self.psum[j][:, 0:512],
                                wd[:, j * 128:(j + 1) * 128],
                                self.hbuf[:, i * 512:(i + 1) * 512],
                                i == 0, False, "ts" if j == 3 else None)
                    ts_gate[i] = p.cnt["ts"]
                for j in range(4):
                    self.mm(self.psum[j][:, 0:512], self.mrow[:, :], hs,
                            False, True, "ts" if j == 3 else None)
                p.wait("vector", "ts")
                for j in range(4):
                    f = fg * 4 + j
                    dst = self.xres[:, f * TOK + T * TT: f * TOK + T * TT + TT]
                    p.op("vector", lambda e, dst=dst, j=j, f=f: e.scalar_tensor_tensor(
                        dst, self.psum[j][:, 0:512], self.sc_d[:, f:f + 1], dst,
                        ALU.mult, ALU.add), "vs")
                p.wait("tensor", "vs")

    def emit_ffn_gelu(self):
        p = self.p
        for T in range(NTT):
            self.emit_layernorm(T)
            for s in ("lw0", "lw1"):
                self.p.wait("sync", s)
            ss_gate = {}
            for i in range(64):
                par = i % 2
                w0 = self.wt[:, par * 2048:(par + 1) * 2048]
                lsem = "lw0" if par == 0 else "lw1"
                if i >= 2:
                    p.wait("sync", "ss", ss_gate[i - 2])
                p.dma("sync", w0.rearrange("p (k o) -> p k o", o=128),
                      self.f1w0T[:, i * 128:(i + 1) * 128].rearrange("(k p) o -> p k o", p=128),
                      lsem)
                ps = self.psum[par][:, 0:512]
                p.wait("tensor", lsem)
                if i >= 2:
                    p.wait("tensor", "ss", ss_gate[i - 2])
                for k in range(NCH):
                    self.mm(ps, w0[:, k * 128:(k + 1) * 128],
                            self.xln[:, k * TT:(k + 1) * TT], k == 0, k == NCH - 1,
                            "ts" if k == NCH - 1 else None)
                ht = self.hbuf[:, i * 512:(i + 1) * 512]
                p.wait("scalar", "ts")
                p.op("scalar", lambda e, ht=ht, ps=ps, i=i: e.activation(
                    ht, ps, AF.Gelu, bias=self.b0pp[:, i:i + 1]), "ss")
                ss_gate[i] = p.cnt["ss"]
            p.wait("tensor", "ss")
            for fg in range(4):
                ts_gate = {}
                for s in ("lw0", "lw1"):
                    p.wait("sync", s)
                p.wait("sync", "ts")
                for i in range(64):
                    par = i % 2
                    w1 = self.wt[:, 8192 + par * 512: 8192 + (par + 1) * 512]
                    lsem = "lw0" if par == 0 else "lw1"
                    if i >= 2:
                        p.wait("sync", "ts", ts_gate[i - 2])
                    p.dma("sync", w1,
                          self.f1w1T[i * 128:(i + 1) * 128, fg * 512:(fg + 1) * 512],
                          lsem)
                    p.wait("tensor", lsem)
                    if i == 0:
                        p.wait("tensor", "vs")
                    for j in range(4):
                        self.mm(self.psum[j][:, 0:512],
                                w1[:, j * 128:(j + 1) * 128],
                                self.hbuf[:, i * 512:(i + 1) * 512],
                                i == 0, False, "ts" if j == 3 else None)
                    ts_gate[i] = p.cnt["ts"]
                for j in range(4):
                    f = fg * 4 + j
                    self.mm(self.psum[j][:, 0:512],
                            self.b1row[0:1, f * 128:(f + 1) * 128],
                            self.ones_row[:, :], False, True,
                            "ts" if j == 3 else None)
                p.wait("vector", "ts")
                for j in range(4):
                    f = fg * 4 + j
                    dst = self.xres[:, f * TOK + T * TT: f * TOK + T * TT + TT]
                    p.op("vector", lambda e, dst=dst, j=j: e.scalar_tensor_tensor(
                        dst, self.psum[j][:, 0:512], 0.5, dst,
                        ALU.mult, ALU.add), "vs")
                p.wait("tensor", "vs")


    def emit_collective(self, kind, in_ap, out_ap, alu="bypass"):
        p = self.p
        from concourse.aot_env import allow_ici_all_to_all

        def f(e):
            with allow_ici_all_to_all():
                return e.collective_compute(
                    kind, getattr(ALU, alu), replica_groups=[list(range(NCORE))],
                    ins=[in_ap], outs=[out_ap])
        return p.op("gpsimd", f, "cc", 1)

    def emit_attn(self):
        p = self.p
        self.cmask = self.prs[:, 1920:2816]
        p.wait("sync", "sp0")
        p.wait("sync", "sp1")
        p.dma("sync", self.cmask, self.c_cmask, "lsm")
        p.wait("vector", "lsm")
        H0 = 0
        Q_ = [0, 8192]
        K_ = [16384, 24576]
        for T in range(NTT):
            if T > 0:
                p.wait("vector", "st0")
            self.emit_layernorm(T)
            p.wait("sync", "vs")
            p.dma("sync",
                  self.gb_xln_in[:, T * TT:(T + 1) * TT].rearrange("(c p) t -> p c t", p=128),
                  self.xln[:, :].rearrange("p (c t) -> p c t", t=TT), "st0")
        p.wait("gpsimd", "st0")
        cc_x = self.emit_collective("AllGather", self.gb_xln_in.opt(), self.gb_xln_out.opt())
        p.wait("sync", "sp0")
        p.wait("sync", "sp1")
        p.dma("sync", self.wt[:, 0:12288].rearrange("p (k o) -> p k o", o=768),
              self.qkvT.rearrange("(k p) o -> p k o", p=128), "lw0")
        qkv_ld = p.cnt["lw0"]
        vs_gate = {}
        ts_gate = {}
        st_par = ["st0", "st1"]
        for tg in range(16):
            r, tl = tg // 2, (tg % 2) * TT
            p.wait("sync", "cc", cc_x)
            if tg >= 1:
                p.wait("sync", "ts", ts_gate[tg - 1])
            p.dma("sync", self.xln[:, :].rearrange("p (c t) -> p c t", t=TT),
                  self.gb_xln_out[2048 * r:2048 * (r + 1), tl:tl + TT]
                  .rearrange("(c p) t -> p c t", p=128), "lw1")
            if tg >= 1:
                p.wait("sync", "vs", vs_gate[tg - 1])
            p.dma("sync", self.scr[0:64, 1024:1536],
                  self.c_cos[:, tg * TT:(tg + 1) * TT], "lx")
            p.dma("sync", self.scr[0:64, 1536:2048],
                  self.c_sin[:, tg * TT:(tg + 1) * TT], "lx")
            p.wait("tensor", "lw0", qkv_ld)
            p.wait("tensor", "lw1")
            if tg >= 1:
                p.wait("tensor", "vs", vs_gate[tg - 1])
            for m in range(2):
                for h in range(2):
                    ps = self.psum[m * 2 + h][:, 0:512]
                    for kk in range(NCH):
                        self.mm(ps, self.wt[:, kk * 768 + m * 256 + h * 128:
                                            kk * 768 + m * 256 + (h + 1) * 128],
                                self.xln[:, kk * TT:(kk + 1) * TT],
                                kk == 0, kk == NCH - 1,
                                "ts" if (kk == NCH - 1) else None)
            qk_ts = p.cnt["ts"]
            for vt in range(4):
                ps = self.psum[4 + vt % 2][:, 0:256]
                pvt = (tg, vt - 2) if vt >= 2 else (tg - 1, vt + 2)
                if pvt in self.at_vss:
                    p.wait("tensor", "ss", self.at_vss[pvt])
                for kk in range(NCH):
                    self.mm(ps, self.xln[:, kk * TT + vt * 128: kk * TT + (vt + 1) * 128],
                            self.wt[:, kk * 768 + 512: kk * 768 + 768],
                            kk == 0, kk == NCH - 1,
                            "ts" if kk == NCH - 1 else None)
                sdst = self.scrb[:, (vt % 2) * 256:(vt % 2) * 256 + 256]
                p.wait("scalar", "ts")
                if tg > 0 or vt >= 2:
                    p.wait("scalar", st_par[vt % 2])
                p.op("scalar", lambda e, sdst=sdst, ps=ps: e.activation(sdst, ps, AF.Copy), "ss")
                self.at_vss[(tg, vt)] = p.cnt["ss"]
                p.wait("sync", "ss")
                p.dma("sync", self.vsc[tg * TT + vt * 128: tg * TT + (vt + 1) * 128, :],
                      sdst, st_par[vt % 2])
            ts_gate[tg] = p.cnt["ts"]
            p.wait("vector", "ts", qk_ts)
            p.wait("vector", "lx")
            cos = self.scr[0:64, 1024:1536]
            sin = self.scr[0:64, 1536:2048]
            for m in range(2):
                for h in range(2):
                    ps = self.psum[m * 2 + h]
                    dst = self.hbuf[:, (Q_[h] if m == 0 else K_[h]) + tg * TT:
                                    (Q_[h] if m == 0 else K_[h]) + (tg + 1) * TT]
                    a = self.scr[0:64, 0:512]
                    b = self.scr[0:64, 512:1024]
                    lo, hi = ps[0:64, 0:512], ps[64:128, 0:512]
                    if m + h > 0:
                        p.wait("vector", "vs")
                    p.op("vector", lambda e, a=a, lo=lo, cos=cos: e.tensor_mul(a, lo, cos), "vs")
                    p.op("vector", lambda e, b=b, hi=hi, sin=sin: e.tensor_mul(b, hi, sin), "vs")
                    p.wait("vector", "vs")
                    p.op("vector", lambda e, dst=dst, a=a, b=b: e.tensor_sub(dst[0:64, :], a, b), "vs")
                    p.wait("vector", "vs")
                    p.op("vector", lambda e, a=a, lo=lo, sin=sin: e.tensor_mul(a, lo, sin), "vs")
                    p.op("vector", lambda e, b=b, hi=hi, cos=cos: e.tensor_mul(b, hi, cos), "vs")
                    p.wait("vector", "vs")
                    p.op("vector", lambda e, dst=dst, a=a, b=b: e.tensor_add(dst[64:128, :], a, b), "vs")
            vs_gate[tg] = p.cnt["vs"]
            self.at_ss_gate[tg] = p.cnt["ss"]
        p.wait("sync", "st0")
        p.wait("sync", "st1")
        p.wait("sync", "lw2")
        p.wait("sync", "lw3")
        p.wait("tensor", "ss")
        p.wait("tensor", "vs")
        lsems = ["lw2", "lw3"]
        for h in range(2):
            for qt in range(16):
                nkt = 4 * qt + 4
                ps_o = self.psum[4][:, 0:512]
                ps_d = self.psum[6][0:1, 0:512]
                ps_r = self.psum[7][:, 0:512]
                vinfo = {}
                for kt in range(nkt + 1):
                    if kt < nkt:
                        par = kt % 2
                        ps_s = self.psum[par][:, 0:512]
                        s2 = self.scr[:, par * 512:(par + 1) * 512]
                        s4 = self.xln[:, par * 512:(par + 1) * 512]
                        vslot = self.wt[:, 12288 + par * 128: 12288 + (par + 1) * 128]
                        if self.at_tq.get("last_" + str(par)) is not None:
                            p.wait("sync", "tq", self.at_tq["last_" + str(par)])
                        p.dma("sync", vslot, self.vsc[kt * 128:(kt + 1) * 128,
                                                      h * 128:(h + 1) * 128], lsems[par])
                        vinfo[kt] = (ps_s, s2, s4, vslot, p.cnt[lsems[par]])
                        if self.at_tq.get("lastq_" + str(par)) is not None:
                            p.wait("tensor", "ss", self.at_tq["lastq_" + str(par)])
                        self.mm(ps_s, self.hbuf[:, K_[h] + kt * 128: K_[h] + (kt + 1) * 128],
                                self.hbuf[:, Q_[h] + qt * TT: Q_[h] + (qt + 1) * TT],
                                True, True, "ts")
                        tsv = p.cnt["ts"]
                        p.wait("scalar", "ts", tsv)
                        if self.at_tq.get("lastv_" + str(par)) is not None:
                            p.wait("scalar", "vs", self.at_tq["lastv_" + str(par)])
                        p.op("scalar", lambda e, s2=s2, ps_s=ps_s: e.activation(s2, ps_s, AF.Square), "ss")
                        ssv = p.cnt["ss"]
                        self.at_tq["lastq_" + str(par)] = ssv
                        p.wait("vector", "ss", ssv)
                        if self.at_tq.get("lasttq_" + str(par)) is not None:
                            p.wait("vector", "tq", self.at_tq["lasttq_" + str(par)])
                        p.op("vector", lambda e, s4=s4, s2=s2: e.tensor_mul(s4, s2, s2), "vs")
                        if kt >= 4 * qt:
                            moff = kt - 4 * qt
                            msk = self.cmask[:, 384 - 128 * moff: 896 - 128 * moff]
                            p.wait("vector", "vs")
                            p.op("vector", lambda e, s4=s4, msk=msk: e.tensor_mul(s4, s4, msk), "vs")
                        self.at_tq["lastv_" + str(par)] = p.cnt["vs"]
                        self.at_tq["vsv_" + str(kt % 2)] = p.cnt["vs"]
                    if kt >= 1:
                        j = kt - 1
                        parj = j % 2
                        _, _, s4j, vslotj, vl = vinfo[j]
                        p.wait("tensor", "vs", self.at_tq["vsv_" + str(parj)])
                        p.wait("tensor", lsems[parj], vl)
                        self.mm(ps_o, vslotj, s4j, j == 0, j == nkt - 1)
                        self.mm(ps_d, self.ones_col[:, :], s4j, j == 0, j == nkt - 1, "tq")
                        self.at_tq["last_" + str(parj)] = p.cnt["tq"]
                        self.at_tq["lasttq_" + str(parj)] = p.cnt["tq"]
                p.wait("vector", "tq")
                p.wait("vector", "st0")
                r1 = self.stat[0:1, 0:512]
                r2 = self.stat[0:1, 512:1024]
                p.op("vector", lambda e, r1=r1, ps_d=ps_d: e.tensor_scalar_max(r1, ps_d, 1e-6), "vs")
                p.wait("vector", "vs")
                p.op("vector", lambda e, r2=r2, r1=r1: e.reciprocal(r2, r1), "vs")
                p.wait("tensor", "vs")
                self.mm(ps_r, self.ones_r128f[:, :], r2, True, True, "tq")
                p.wait("vector", "tq")
                rbc = self.scr[:, 2048:2560]
                p.op("vector", lambda e, rbc=rbc, ps_r=ps_r: e.tensor_copy(rbc, ps_r), "vs")
                p.wait("vector", "vs")
                obf = self.scrb[:, 1024:1536]
                p.op("vector", lambda e, obf=obf, ps_o=ps_o, rbc=rbc: e.tensor_mul(obf, ps_o, rbc), "vs")
                p.wait("sync", "vs")
                j, tl = qt // 2, (qt % 2) * TT
                p.dma("sync", self.gb_ao_in[j * 256 + h * 128: j * 256 + (h + 1) * 128,
                                            tl:tl + TT], obf, "st0")
                p.wait("tensor", "vs")
        p.wait("gpsimd", "st0")
        cc_a = self.emit_collective("AllToAll", self.gb_ao_in.opt(), self.gb_ao_out.opt())
        for T in range(NTT):
            p.wait("sync", "cc", cc_a)
            p.wait("sync", "tq")
            p.dma("sync", self.xln[:, :].rearrange("p (c t) -> p c t", t=TT),
                  self.gb_ao_out[:, T * TT:(T + 1) * TT]
                  .rearrange("(c p) t -> p c t", p=128), "lw1")
            for fg in range(4):
                for s in ("lw0",):
                    p.wait("sync", s)
                p.wait("sync", "tq")
                p.dma("sync", self.wt[:, 0:8192].rearrange("p (k o) -> p k o", o=512),
                      self.woT[:, fg * 512:(fg + 1) * 512].rearrange("(k p) o -> p k o", p=128),
                      "lw0")
                p.wait("tensor", "lw0")
                p.wait("tensor", "lw1")
                p.wait("tensor", "vs")
                for kk in range(NCH):
                    for j in range(4):
                        self.mm(self.psum[j][:, 0:512],
                                self.wt[:, kk * 512 + j * 128: kk * 512 + (j + 1) * 128],
                                self.xln[:, kk * TT:(kk + 1) * TT],
                                kk == 0, kk == NCH - 1,
                                "tq" if (kk == NCH - 1 and j == 3) else None)
                p.wait("vector", "tq")
                for j in range(4):
                    f = fg * 4 + j
                    dst = self.xres[:, f * TOK + T * TT: f * TOK + T * TT + TT]
                    p.op("vector", lambda e, dst=dst, j=j: e.tensor_add(
                        dst, self.psum[j][:, 0:512], dst), "vs")
                p.wait("tensor", "vs")



    def emit_rms_inplace(self, buf, nch, gain, gcol0):
        p = self.p
        ps_q = self.psum[7][0:1, 0:512]
        ps_a = self.psum[7][:, 0:512]
        p.wait("scalar", "vs")
        for s_ in ("lw0", "lw3", "lg0", "lg1"):
            p.wait("scalar", s_)
        gate = {}
        for c in range(nch):
            par = c % 2
            t = buf[:, c * 512:(c + 1) * 512]
            xsq = self.scrb[:, par * 512:(par + 1) * 512]
            if c >= 2:
                p.wait("scalar", "ts", gate[c - 2])
            p.op("scalar", lambda e, xsq=xsq, t=t: e.activation(xsq, t, AF.Square), "ss")
            p.wait("tensor", "ss")
            self.mm(ps_q, self.ones_col[:, :], xsq, c == 0, c == nch - 1, "ts")
            gate[c] = p.cnt["ts"]
        p.wait("vector", "ts")
        s_v = self.stat[0:1, 512:1024]
        p.op("vector", lambda e, s_v=s_v, nch=nch: e.tensor_scalar_mul(
            s_v, ps_q, 1.0 / (nch * 128)), "vs")
        p.wait("scalar", "vs")
        p.op("scalar", lambda e, s_v=s_v: e.activation(
            s_v, s_v, AF.Ln, bias=self.konst[0:1, 0:1]), "ss")
        p.wait("scalar", "ss")
        p.op("scalar", lambda e, s_v=s_v: e.activation(
            s_v, s_v, AF.Exp, scale=self.konst[0:1, 1:2]), "ss")
        p.wait("tensor", "ss")
        self.mm(ps_a, self.ones_r128f[:, :], s_v, True, True, "tq")
        p.wait("vector", "tq")
        for c in range(nch):
            t = buf[:, c * 512:(c + 1) * 512]
            p.wait("vector", "vs")
            if gain is not None:
                g = gain[:, gcol0 + c:gcol0 + c + 1]
                p.op("vector", lambda e, t=t, g=g: e.scalar_tensor_tensor(
                    t, t, g, ps_a, ALU.mult, ALU.mult), "vs")
            else:
                p.op("vector", lambda e, t=t: e.tensor_mul(t, t, ps_a), "vs")
        p.wait("tensor", "vs")

    def emit_mamba(self):
        p = self.p
        xF = self.xln[:, :].bitcast(F32)
        prsF = self.prs[:, :].bitcast(F32)
        HB = self.hbuf
        p.wait("sync", "lsm")
        p.dma("sync", self.prs[:, :].bitcast(F32)[:, 0:160].rearrange("p (c w) -> p c w", w=4),
              self.conv_w.rearrange("(c p) w -> p c w", p=128), "lsm", slow=True)
        p.wait("sync", "vs")
        p.wait("sync", "lx")
        p.dma("sync", self.stat[0:1, 0:64], self.d_vec.unsqueeze(0), "lx")
        p.wait("tensor", "lx")
        p.wait("tensor", "vs")
        self.mm(self.psum[6][:, 0:64], self.ones_r128f[:, :],
                self.stat[0:1, 0:64], True, True, "tq")
        p.wait("vector", "tq")
        p.op("vector", lambda e: e.tensor_copy(
            self.scr[0:64, 2048:2112], self.psum[6][0:64, 0:64]), "vs")
        p.op("vector", lambda e: e.tensor_copy(
            self.scr[64:128, 2112:2176], self.psum[6][0:64, 0:64]), "vs")
        for half in range(2):
            for g in range(8):
                for hh in range(8):
                    h = 8 * g + hh
                    dst = HB[half * 64:half * 64 + 64,
                             16384 + g * 512 + hh * 64: 16384 + g * 512 + (hh + 1) * 64]
                    idn = self.eye128[half * 64:half * 64 + 64, half * 64:half * 64 + 64]
                    drc = self.scr[half * 64:half * 64 + 64, 2048 + half * 0 + 0:2048] if False else None
                    p.wait("vector", "vs")
                    p.op("vector", lambda e, dst=dst, h=h, idn=idn, half=half: e.tensor_scalar_mul(
                        dst, idn, (self.scr[0:64, 2048:2112] if half == 0 else
                                   self.scr[64:128, 2112:2176])[:, h:h + 1]), "vs")
        self.emit_mamba_front()
        self.emit_mamba_pass1()
        self.emit_mamba_pass2()
        self.emit_mamba_mlp()

    def emit_mamba_front(self):
        p = self.p
        F_ = self.scr
        xF = self.xln[:, :].bitcast(F32)
        p.op("vector", lambda e: e.memset(self.prs[:, 1024:1184], 0.0), "vs")
        for T in range(NTT):
            p.wait("vector", "st0")
            p.wait("scalar", "st0")
            p.wait("scalar", "st1")
            self.emit_layernorm(T)
            p.wait("sync", "vs")
            p.wait("sync", "st2")
            p.dma("sync", self.m_lnx[:, T * TT:(T + 1) * TT].rearrange("(c p) t -> p c t", p=128),
                  self.xln[:, :].rearrange("p (c t) -> p c t", t=TT), "st2")
            p.wait("vector", "st2")
            self.emit_rms_inplace(self.xln, NCH, self.g_ln, 0)
            zg = {}
            if T > 0:
                p.wait("gpsimd", "ss")
            for o in range(73):
                par = o % 2
                wi = self.wt[:, par * 2048:(par + 1) * 2048]
                lsem = "lg0" if par == 0 else "lg1"
                ow = 128 if o < 72 else 64
                if o >= 2:
                    p.wait("gpsimd", "ss", zg[o - 2])
                p.dma("gpsimd", wi[:, 0:16 * ow].rearrange("p (k o) -> p k o", o=ow),
                      self.w_in[:, o * 128: o * 128 + ow].rearrange("(k p) o -> p k o", p=128),
                      lsem)
                ps = self.psum[par][:, 0:512]
                p.wait("tensor", lsem)
                if o >= 2:
                    p.wait("tensor", "ss", zg[o - 2])
                for k in range(NCH):
                    self.mm(ps[0:ow, :], wi[:, k * ow:(k + 1) * ow],
                            self.xln[:, k * TT:(k + 1) * TT], k == 0, k == NCH - 1,
                            "ts" if k == NCH - 1 else None)
                sl = self.scrb[:, 1088 + par * 512: 1088 + (par + 1) * 512]
                stsem = "st0" if par == 0 else "st1"
                p.wait("scalar", "ts")
                if o >= 2:
                    p.wait("scalar", stsem)
                if o < 32:
                    p.op("scalar", lambda e, sl=sl, ps=ps: e.activation(sl, ps, AF.Silu), "ss")
                    zg[o] = p.cnt["ss"]
                    p.wait("sync", "ss")
                    p.dma("sync", self.m_z[o * 128:(o + 1) * 128, T * TT:(T + 1) * TT],
                          sl, stsem)
                elif o < 72:
                    p.op("scalar", lambda e, sl=sl, ps=ps: e.activation(sl, ps, AF.Copy), "ss")
                    zg[o] = p.cnt["ss"]
                    oc = o - 32
                    if T == NTT - 1:
                        hs = self.prs[:, 1024 + oc * 4: 1024 + oc * 4 + 3]
                        p.wait("vector", "ss")
                        p.op("vector", lambda e, hs=hs, sl=sl: e.tensor_copy(
                            hs, sl[:, 509:512]), "vs")
                        p.wait("sync", "vs")
                    else:
                        p.wait("sync", "ss")
                    p.dma("sync", self.m_xbc[oc * 128:(oc + 1) * 128, T * TT:(T + 1) * TT],
                          sl, stsem)
                else:
                    p.op("scalar", lambda e, ps=ps, T=T: e.activation(
                        self.scr[0:64, T * 512:(T + 1) * 512], ps[0:64, :], AF.Copy), "ss")
                    zg[o] = p.cnt["ss"]
        p.wait("vector", "ss")
        p.wait("sync", "vs")
        p.wait("sync", "st0")
        p.wait("sync", "st1")
        p.wait("sync", "st2")
        p.dma("sync", self.gb_halo_in.opt(), self.prs[:, 1024:1184], "st2")
        p.wait("gpsimd", "st2")
        self.cc_halo = self.emit_collective("AllGather", self.gb_halo_in.opt(),
                                            self.gb_halo_out.opt())
        A_ = F_[0:64, 0:1024]
        B_ = F_[64:128, 0:1024]
        C_ = F_[64:128, 1024:2048]
        E_ = F_[0:64, 2048:3072]
        G_ = F_[0:64, 1024:2048]
        dtb = self.g_nw[0:64, 32:33]
        p.wait("sync", "ss")
        p.dma("sync", self.m_dbg[:, 2048:3072], A_, "stg")
        p.wait("vector", "stg")
        p.wait("vector", "ss")
        p.op("vector", lambda e: e.tensor_scalar_add(A_, A_, dtb), "vs")
        p.wait("scalar", "vs")
        p.op("scalar", lambda e: e.activation(G_, A_, AF.Abs), "ss")
        p.wait("scalar", "ss")
        p.op("scalar", lambda e: e.activation(G_, G_, AF.Exp, scale=self.konst[0:64, 3:4]), "ss")
        p.wait("scalar", "ss")
        p.op("scalar", lambda e: e.activation(G_, G_, AF.Ln, bias=self.konst[0:64, 4:5]), "ss")
        p.wait("vector", "ss")
        p.op("vector", lambda e: e.tensor_relu(E_, A_), "vs")
        p.wait("vector", "vs")
        p.op("vector", lambda e: e.tensor_add(A_, E_, G_), "vs")
        p.wait("sync", "vs")
        p.dma("sync", self.m_dbg[:, 0:1024], A_, "stg")
        p.wait("vector", "stg")
        p.wait("scalar", "stg")
        p.wait("scalar", "vs")
        p.op("scalar", lambda e: e.activation(C_, A_, AF.Ln), "ss")
        p.op("scalar", lambda e: e.activation(self.g_nw[0:64, 34:35],
                                              self.g_nw[0:64, 33:34], AF.Exp), "ss")
        p.wait("vector", "ss")
        p.op("vector", lambda e: e.tensor_scalar_mul(self.g_nw[0:64, 34:35],
                                                     self.g_nw[0:64, 34:35], -1.0), "vs")
        p.wait("vector", "vs")
        p.op("vector", lambda e: e.tensor_scalar_mul(G_, A_, self.g_nw[0:64, 34:35]), "vs")
        p.wait("sync", "vs")
        p.dma("sync", self.m_dbg[:, 1024:2048], G_, "stg")
        p.wait("vector", "stg")
        for c in range(N_CHUNK):
            p.wait("vector", "vs")
            p.op("vector", lambda e, c=c: e.tensor_tensor_scan(
                B_[:, c * 64:(c + 1) * 64], G_[:, c * 64:(c + 1) * 64],
                G_[:, c * 64:(c + 1) * 64], 0.0, ALU.add, ALU.bypass), "vs")
        p.wait("sync", "vs")
        p.wait("sync", "st2")
        p.dma("sync", self.m_acum[:, :], B_, "st2")
        for c in range(N_CHUNK):
            last = B_[:, c * 64 + 63:c * 64 + 64]
            p.wait("vector", "vs")
            p.op("vector", lambda e, c=c, last=last: e.tensor_scalar(
                E_[:, c * 64:(c + 1) * 64], B_[:, c * 64:(c + 1) * 64], last, None,
                ALU.subtract), "vs")
        p.wait("scalar", "vs")
        p.op("scalar", lambda e: e.activation(E_, E_, AF.Exp, scale=self.konst[0:64, 3:4]), "ss")
        p.wait("vector", "ss")
        p.op("vector", lambda e: e.tensor_mul(E_, E_, A_), "vs")
        p.wait("vector", "vs")
        p.op("vector", lambda e: e.tensor_sub(G_, C_, B_), "vs")
        p.wait("sync", "vs")
        p.wait("sync", "st2")
        p.dma("sync", self.m_rows[:, 0:1024], G_, "st2")
        p.wait("scalar", "st2")
        p.wait("scalar", "vs")
        p.op("scalar", lambda e: e.activation(A_, B_, AF.Exp), "ss")
        p.wait("sync", "ss")
        p.wait("sync", "st2")
        p.dma("sync", self.m_od[:, :], A_, "st2")
        p.wait("vector", "st2")
        p.op("vector", lambda e: e.tensor_copy(G_[:, 0:16], B_[:, 63:1024:64]), "vs")
        p.wait("scalar", "vs")
        p.op("scalar", lambda e: e.activation(G_[:, 0:16], G_[:, 0:16], AF.Exp), "ss")
        p.wait("vector", "ss")
        p.op("vector", lambda e: e.tensor_tensor_scan(
            G_[:, 16:32], G_[:, 0:16], G_[:, 0:16], 1.0, ALU.mult, ALU.bypass), "vs")
        p.wait("vector", "vs")
        p.op("vector", lambda e: e.memset(G_[:, 32:33], 1.0), "vs")
        p.wait("vector", "vs")
        p.op("vector", lambda e: e.tensor_copy(G_[:, 33:48], G_[:, 16:31]), "vs")
        p.wait("sync", "vs")
        p.wait("sync", "st2")
        p.dma("sync", self.m_rows3[:, 0:16], G_[:, 0:16], "st2")
        p.wait("sync", "st2")
        p.dma("sync", self.m_rows3[:, 16:32], G_[:, 32:48], "st2")
        p.wait("vector", "st2")
        p.op("vector", lambda e: e.tensor_copy(self.g_nw[0:64, 35:36], G_[:, 31:32]), "vs")
        stg = xF[0:128, 3072:3136]
        p.wait("tensor", "vs")
        p.wait("tensor", "ss")
        for src_, dst, idn in ((C_, self.m_rows2, self.eye128[64:128, 64:128]),
                               (E_, self.m_rows2w, self.eye128[0:64, 0:64])):
            for j in range(8):
                pb = self.psum[6][:, 0:64]
                p.wait("tensor", "st2")
                p.op("tensor", lambda e, j=j, pb=pb, src_=src_, idn=idn: e.transpose(
                    pb, src_[:, j * 128:(j + 1) * 128], idn), "tp")
                p.wait("vector", "tp")
                p.op("vector", lambda e, pb=pb, stg=stg: e.tensor_copy(stg, pb), "vs")
                p.wait("sync", "vs")
                p.wait("sync", "st2")
                p.dma("sync", dst[j * 128:(j + 1) * 128, :], stg, "st2")
        p.wait("sync", "st2")
        for half in range(2):
            p.dma("sync", xF[half * 64:half * 64 + 64, 2048:3072],
                  self.m_rows2.rearrange("(c s) h -> s c h", s=64), "lx")
            p.dma("sync", xF[half * 64:half * 64 + 64, 3072:4096],
                  self.m_rows2w.rearrange("(c s) h -> s c h", s=64), "lx")
        p.wait("sync", "st1")
        for what, c0, dcol in ((0, 0, 1024), (1, 16, 0)):
            p.wait("sync", "tq")
            p.wait("sync", "lx")
            p.dma("sync", self.stat[0:1, 0:1024].rearrange("o (h c) -> o h c", c=16),
                  self.m_rows3[:, c0:c0 + 16].unsqueeze(0), "lx")
            p.wait("tensor", "lx")
            p.wait("tensor", "vs")
            for blk in range(2):
                self.mm(self.psum[6][:, 0:512], self.ones_r128f[:, :],
                        self.stat[0:1, blk * 512:(blk + 1) * 512], True, True, "tq")
                p.wait("vector", "tq")
                dst = self.scr[:, dcol + blk * 512: dcol + (blk + 1) * 512]
                p.op("vector", lambda e, dst=dst: e.tensor_copy(dst, self.psum[6][:, 0:512]), "vs")
                p.wait("tensor", "vs")
        p.wait("gpsimd", "lx")


    def emit_mamba_pass1(self):
        p = self.p
        xF = self.xln[:, :].bitcast(F32)
        prsF = self.prs[:, :].bitcast(F32)
        HB = self.hbuf
        carry = xF[:, 0:2048]
        self.mb_cg = {}
        p.wait("vector", "lx")
        p.op("vector", lambda e, carry=carry: e.memset(carry, 0.0), "vs")
        acc = self.prs[:, 1184:1344]
        p.op("vector", lambda e, acc=acc: e.memset(acc, 0.0), "vs")
        p.wait("sync", "cc", self.cc_halo)
        for j in range(NCORE):
            slot = self.prs[:, 1344:1504]
            p.wait("sync", "vs")
            p.dma("sync", slot, self.gb_halo_out[j * 128:(j + 1) * 128, :], "lx")
            p.wait("vector", "lx")
            pm = self.pmask[:, j:j + 1]
            p.wait("vector", "vs")
            p.op("vector", lambda e, slot=slot, pm=pm, acc=acc: e.scalar_tensor_tensor(
                acc, slot, pm, acc, ALU.mult, ALU.add), "vs")
        p.wait("vector", "vs")
        p.op("vector", lambda e, prsF=prsF: e.memset(prsF[64:66, 160:1184], 1.0), "vs")
        for T in range(NTT):
            for oc in range(40):
                par = oc % 2
                ci = self.scrb[:, par * 516:par * 516 + 516]
                lsem = "lw0" if par == 0 else "lw1"
                if (T, oc) != (0, 0) and (T, oc) != (0, 1):
                    p.wait("sync", "ss")
                if T == 0:
                    p.dma("sync", ci[:, 3:515],
                          self.m_xbc[oc * 128:(oc + 1) * 128, 0:512], lsem)
                    p.wait("vector", lsem)
                    p.wait("vector", "vs")
                    p.op("vector", lambda e, ci=ci, oc=oc, acc=acc: e.tensor_copy(
                        ci[:, 0:3], acc[:, oc * 4:oc * 4 + 3]), "vs")
                    p.op("vector", lambda e, ci=ci: e.memset(ci[:, 515:516], 0.0), "vs")
                else:
                    p.dma("sync", ci[:, 0:515],
                          self.m_xbc[oc * 128:(oc + 1) * 128, T * 512 - 3:(T + 1) * 512], lsem)
                    p.wait("vector", lsem)
                p.wait("vector", "ts")
                p.wait("vector", "lsm")
                for j in range(4):
                    dg = self.wt[:, 8192 + j * 128: 8192 + (j + 1) * 128]
                    cwc = self.prs[:, :].bitcast(F32)[:, oc * 4 + j: oc * 4 + j + 1]
                    p.wait("vector", "vs")
                    p.op("vector", lambda e, dg=dg, cwc=cwc: e.tensor_scalar_mul(
                        dg, self.eye128[:, :], cwc), "vs")
                ps = self.psum[4][:, 0:512]
                p.wait("tensor", "vs")
                p.wait("tensor", "ss")
                for j in range(4):
                    self.mm(ps, self.wt[:, 8192 + j * 128: 8192 + (j + 1) * 128],
                            ci[:, j:j + 512], j == 0, j == 3, "ts" if j == 3 else None)
                p.wait("scalar", "ts")
                stsem = "st0" if par == 0 else "st1"
                if oc < 4:
                    dst = self.wt[:, oc * 1024 + T * 512: oc * 1024 + T * 512 + 512]
                    p.op("scalar", lambda e, dst=dst, ps=ps, oc=oc: e.activation(
                        dst, ps, AF.Silu, bias=self.cbpp[:, oc:oc + 1]), "ss")
                elif oc < 8:
                    dst = self.wt[:, 4096 + (oc - 4) * 1024 + T * 512: 4096 + (oc - 4) * 1024 + T * 512 + 512]
                    p.op("scalar", lambda e, dst=dst, ps=ps, oc=oc: e.activation(
                        dst, ps, AF.Silu, bias=self.cbpp[:, oc:oc + 1]), "ss")
                else:
                    sl = self.scrb[:, 1088 + par * 512: 1088 + (par + 1) * 512]
                    p.wait("scalar", stsem)
                    p.op("scalar", lambda e, sl=sl, ps=ps, oc=oc: e.activation(
                        sl, ps, AF.Silu, bias=self.cbpp[:, oc:oc + 1]), "ss")
                    p.wait("sync", "ss")
                    p.dma("sync", self.m_cT[(oc - 8) * 128:(oc - 7) * 128, T * 512:(T + 1) * 512],
                          sl, stsem)
            p.wait("tensor", "ss")
            psb = self.psum[6][:, 0:512].bitcast(BF16)[:, 0:128]
            for which, dst0 in ((0, 20480), (1, 22528)):
                for fc in range(4):
                    for tt in range(4):
                        srcap = self.wt[:, which * 4096 + fc * 1024 + T * 512 + tt * 128:
                                        which * 4096 + fc * 1024 + T * 512 + (tt + 1) * 128]
                        p.wait("tensor", "vs")
                        p.op("tensor", lambda e, srcap=srcap, psb=psb: e.transpose(
                            psb, srcap, self.eye128b[:, :]), "tp")
                        p.wait("vector", "tp")
                        dst = HB[:, dst0 + tt * 512 + fc * 128: dst0 + tt * 512 + (fc + 1) * 128]
                        p.op("vector", lambda e, dst=dst, psb=psb: e.tensor_copy(dst, psb), "vs")
            p.wait("sync", "st0")
            p.wait("sync", "st1")
            if T > 0:
                p.wait("sync", "ts")
            for plane in range(2):
                for gg in range(4):
                    p.dma("sync", HB[plane * 64:(plane + 1) * 64, gg * 4096:(gg + 1) * 4096]
                          .rearrange("p (h l) -> p h l", h=8),
                          self.m_cT[:, T * 512:(T + 1) * 512]
                          .rearrange("(g q h n) l -> q g n h l", g=4, q=2, h=8)[plane, gg],
                          "lw2")
            p.wait("tensor", "lw2")
            for c8 in range(8):
                c = T * 8 + c8
                cb = (c8 % 2) * 64
                p.wait("sync", "vs")
                p.dma("sync", self.m_prev[c * 128:(c + 1) * 128, :], carry, "st1")
                for g in range(8):
                    plane = g % 2
                    gg = g // 2
                    pb_ = plane * 64
                    p.wait("sync", "tq")
                    p.dma("sync", prsF[64:66, 160:672].rearrange("o (h s) -> o h s", h=8)[1:2],
                          self.m_rows[:, :].rearrange("(hh h) t -> hh h t", hh=8)
                          [g:g + 1, :, c * 64:(c + 1) * 64], "lx")
                    p.dma("sync", prsF[64:66, 672:1184].rearrange("o (h s) -> o h s", h=8)[0:1],
                          self.m_acum[:, :].rearrange("(hh h) t -> hh h t", hh=8)
                          [g:g + 1, :, c * 64:(c + 1) * 64], "lx")
                    p.wait("tensor", "lx")
                    bch = g // 2
                    bbase = (g % 2) * 64
                    lhsB = self.wt[bbase:bbase + 64,
                                   4096 + bch * 1024 + T * 512 + c8 * 64:
                                   4096 + bch * 1024 + T * 512 + c8 * 64 + 64]
                    rhsC = HB[pb_:pb_ + 64, gg * 4096: (gg + 1) * 4096] \
                        .rearrange("n (h l) -> n h l", h=8)[:, :, c8 * 64:(c8 + 1) * 64]
                    p.wait("tensor", "vs")
                    self.mm(self.psum[0][0:64, 0:512], lhsB, rhsC, True, True, "ts")
                    for hh in range(8):
                        self.mm(self.psum[1][0:64, hh * 64:(hh + 1) * 64],
                                prsF[64:66, 160 + hh * 64: 160 + (hh + 1) * 64],
                                prsF[64:66, 672 + hh * 64: 672 + (hh + 1) * 64],
                                True, True, "ts" if hh == 7 else None)
                    tsv = p.cnt["ts"]
                    EMf = self.scr[cb:cb + 64, 2112:2624]
                    EMb = self.scrb[cb:cb + 64, 0:512]
                    Tpp = self.scrb[cb:cb + 64, 512:1024]
                    p.wait("vector", "ts", tsv)
                    for hh in range(8):
                        h = 8 * g + hh
                        cap = xF[0:64, 2048 + c * 64 + h: 2048 + c * 64 + h + 1]
                        p.wait("vector", "vs")
                        p.op("vector", lambda e, EMf=EMf, hh=hh, cap=cap: e.tensor_scalar_min(
                            EMf[:, hh * 64:(hh + 1) * 64],
                            self.psum[1][0:64, hh * 64:(hh + 1) * 64], cap), "vs")
                    p.wait("scalar", "vs")
                    p.op("scalar", lambda e, EMb=EMb, EMf=EMf: e.activation(EMb, EMf, AF.Exp), "ss")
                    p.wait("vector", "ss")
                    trl = self.tril64[cb:cb + 64, :]
                    p.op("vector", lambda e, EMb=EMb, trl=trl: e.tensor_mul(
                        EMb.rearrange("n (h l) -> n h l", h=8),
                        EMb.rearrange("n (h l) -> n h l", h=8),
                        trl.unsqueeze(1).broadcast_to([64, 8, 64])), "vs")
                    p.wait("vector", "vs")
                    p.op("vector", lambda e, Tpp=Tpp, EMb=EMb: e.tensor_mul(
                        Tpp, self.psum[0][0:64, 0:512], EMb), "vs")
                    p.wait("vector", "vs")
                    dimg = HB[cb:cb + 64, 16384 + g * 512:16384 + (g + 1) * 512]
                    p.op("vector", lambda e, Tpp=Tpp, dimg=dimg: e.tensor_add(Tpp, Tpp, dimg), "vs")
                    tppv = p.cnt["vs"]
                    xts = HB[cb:cb + 64, 20480 + (c8 // 2) * 512 + g * 64:
                             20480 + (c8 // 2) * 512 + (g + 1) * 64]
                    p.wait("tensor", "vs", tppv)
                    self.mm(self.psum[2][0:64, 0:512], xts, Tpp, True, True)
                    Xw = self.scrb[cb:cb + 64, 1088:1600]
                    for hh in range(8):
                        h = 8 * g + hh
                        w = xF[0:64, 3072 + c * 64 + h: 3072 + c * 64 + h + 1]
                        p.wait("vector", "vs")
                        p.op("vector", lambda e, Xw=Xw, hh=hh, xts=xts, w=w: e.tensor_scalar_mul(
                            Xw[:, hh * 64:(hh + 1) * 64], xts, w), "vs")
                    xwv = p.cnt["vs"]
                    btx = HB[cb:cb + 64, 22528 + (c8 // 2) * 512 + g * 64:
                             22528 + (c8 // 2) * 512 + (g + 1) * 64]
                    p.wait("tensor", "vs", xwv)
                    self.mm(self.psum[3][0:64, 0:512], btx, Xw, True, True, "tq")
                    stv = p.cnt["tq"]
                    yst = self.scrb[cb:cb + 64, 1600:2112]
                    p.wait("vector", "tq", stv)
                    p.wait("vector", "st0")
                    p.op("vector", lambda e, yst=yst: e.tensor_copy(
                        yst, self.psum[2][0:64, 0:512]), "vs")
                    p.wait("sync", "vs")
                    dstY = self.m_yd2[pb_:pb_ + 64, T * 16384 + gg * 4096: T * 16384 + (gg + 1) * 4096] \
                        .rearrange("n (h l) -> n h l", h=8)[:, :, c8 * 64:(c8 + 1) * 64]
                    p.dma("sync", dstY, yst.rearrange("n (h l) -> n h l", h=8), "st0")
                    cg = carry[pb_:pb_ + 64, gg * 512:(gg + 1) * 512]
                    p.wait("vector", "st1")
                    for hh in range(8):
                        h = 8 * g + hh
                        dcol = self.scr[pb_:pb_ + 64, 1024 + h * 16 + c: 1024 + h * 16 + c + 1]
                        p.wait("vector", "vs")
                        p.op("vector", lambda e, cg=cg, hh=hh, dcol=dcol: e.tensor_scalar_mul(
                            cg[:, hh * 64:(hh + 1) * 64], cg[:, hh * 64:(hh + 1) * 64], dcol), "vs")
                    p.wait("vector", "vs")
                    p.op("vector", lambda e, cg=cg: e.tensor_add(
                        cg, cg, self.psum[3][0:64, 0:512]), "vs")
        p.wait("sync", "vs")
        p.wait("sync", "st0")
        p.wait("sync", "st1")
        p.wait("sync", "st2")
        p.dma("sync", self.gb_seg_in[:, 0:2048], carry, "st2")
        p.wait("sync", "st2")
        p.dma("sync", self.m_rows3[:, 16:17], self.g_nw[0:64, 35:36], "st2", slow=True)
        p.wait("sync", "st2")
        p.dma("sync", self.stat[0:1, 1024:1088],
              self.m_rows3[:, 16:17].rearrange("h c -> c h"), "lx", slow=True)
        onesq = self.scrb[0:64, 0:64]
        dstage = self.xln[:, :].bitcast(F32)[0:64, 3584:4096]
        dcol = self.xln[:, :].bitcast(F32)[0:64, 3072:3073]
        p.wait("vector", "st2")
        p.op("vector", lambda e, onesq=onesq: e.memset(onesq, 1.0), "vs")
        p.wait("tensor", "lx")
        for plane in range(2):
            for gg in range(4):
                for hh in range(8):
                    h = 16 * gg + 8 * plane + hh
                    p.wait("tensor", "vs")
                    self.mm(self.psum[6][0:64, 0:1], self.ones_r128f[0:1, 0:64],
                            self.stat[0:1, 1024 + h:1025 + h], True, True, "tq")
                    p.wait("vector", "tq")
                    p.op("vector", lambda e, dcol=dcol: e.tensor_copy(
                        dcol, self.psum[6][0:64, 0:1]), "vs")
                    p.wait("vector", "vs")
                    p.op("vector", lambda e, hh=hh, dcol=dcol, dstage=dstage, onesq=onesq:
                         e.tensor_scalar_mul(dstage[:, hh * 64:(hh + 1) * 64], onesq, dcol), "vs")
                p.wait("sync", "vs")
                p.wait("sync", "st2")
                p.dma("sync", self.gb_seg_in[plane * 64:(plane + 1) * 64,
                                             2048 + gg * 512: 2048 + (gg + 1) * 512],
                      dstage, "st2")
                p.wait("vector", "st2")
        p.wait("gpsimd", "st0")
        p.wait("gpsimd", "st1")
        p.wait("gpsimd", "st2")
        self.cc_seg = self.emit_collective("AllGather", self.gb_seg_in.opt(),
                                           self.gb_seg_out.opt())

    def emit_mamba_pass2(self):
        p = self.p
        xF = self.xln[:, :].bitcast(F32)
        HB = self.hbuf
        acc = xF[:, 0:2048]
        Pbuf = xF[:, 2048:4096]
        p.wait("sync", "cc", self.cc_seg)
        p.wait("vector", "vs")
        p.op("vector", lambda e, acc=acc: e.memset(acc, 0.0), "vs")
        for j in range(NCORE):
            p.wait("sync", "vs")
            p.dma("sync", Pbuf, self.gb_seg_out[j * 128:(j + 1) * 128, 2048:4096], "lx")
            p.wait("vector", "lx")
            rm = self.rmask[:, j:j + 1]
            ri = self.rinv[:, j:j + 1]
            p.wait("vector", "vs")
            p.op("vector", lambda e, Pbuf=Pbuf, rm=rm, ri=ri: e.tensor_scalar(
                Pbuf, Pbuf, rm, ri, ALU.mult, ALU.add), "vs")
            p.wait("vector", "vs")
            p.op("vector", lambda e, acc=acc, Pbuf=Pbuf: e.tensor_mul(acc, acc, Pbuf), "vs")
            p.wait("sync", "vs")
            p.dma("sync", Pbuf, self.gb_seg_out[j * 128:(j + 1) * 128, 0:2048], "lx")
            p.wait("vector", "lx")
            p.op("vector", lambda e, acc=acc, Pbuf=Pbuf, rm=rm: e.scalar_tensor_tensor(
                acc, Pbuf, rm, acc, ALU.mult, ALU.add), "vs")
        for T in range(NTT):
            p.wait("sync", "ts")
            for plane in range(2):
                for gg in range(4):
                    p.dma("sync", HB[plane * 64:(plane + 1) * 64, gg * 4096:(gg + 1) * 4096]
                          .rearrange("p (h l) -> p h l", h=8),
                          self.m_cT[:, T * 512:(T + 1) * 512]
                          .rearrange("(g q h n) l -> q g n h l", g=4, q=2, h=8)[plane, gg],
                          "lw2")
            p.wait("gpsimd", "st0")
            p.wait("gpsimd", "st1")
            p.wait("vector", "lw2")
            for blk in range(32):
                par = blk % 2
                gg, hh8 = blk // 8, blk % 8
                odb = self.scrb[:, 1088 + par * 512: 1088 + (par + 1) * 512]
                for plane in range(2):
                    h = 16 * gg + 8 * plane + hh8
                    srcb = self.m_od[h:h + 1, T * 512:(T + 1) * 512].broadcast_to([64, 512])
                    p.wait("gpsimd", "vs")
                    p.dma("gpsimd", odb[plane * 64:plane * 64 + 64, :], srcb, "lgs")
                p.wait("vector", "lgs")
                dst = HB[:, blk * 512:(blk + 1) * 512]
                p.wait("vector", "vs")
                p.op("vector", lambda e, dst=dst, odb=odb: e.tensor_mul(dst, dst, odb), "vs")
            for c8 in range(8):
                c = T * 8 + c8
                p.wait("sync", "vs")
                p.wait("sync", "tq")
                p.dma("sync", Pbuf, self.m_prev[c * 128:(c + 1) * 128, :], "lx")
                p.wait("vector", "lx")
                for g in range(8):
                    plane, gg = g % 2, g // 2
                    pb_ = plane * 64
                    for hh in range(8):
                        h = 8 * g + hh
                        pe = self.scr[pb_:pb_ + 64, h * 16 + c: h * 16 + c + 1]
                        sl = (slice(pb_, pb_ + 64), slice(gg * 512 + hh * 64, gg * 512 + (hh + 1) * 64))
                        p.wait("vector", "vs")
                        p.op("vector", lambda e, sl=sl, pe=pe: e.scalar_tensor_tensor(
                            Pbuf[sl[0], sl[1]], acc[sl[0], sl[1]], pe,
                            Pbuf[sl[0], sl[1]], ALU.mult, ALU.add), "vs")
                Pb16 = HB[:, 16384:18432]
                p.wait("vector", "vs")
                p.op("vector", lambda e, Pb16=Pb16, Pbuf=Pbuf: e.tensor_copy(Pb16, Pbuf), "vs")
                pv = p.cnt["vs"]
                for g in range(8):
                    plane, gg = g % 2, g // 2
                    pb_ = plane * 64
                    cb = (c8 % 2) * 64
                    p.wait("tensor", "vs", pv)
                    if True:
                        p.wait("tensor", "ss")
                    for hh in range(8):
                        lhsP = Pb16[pb_:pb_ + 64, gg * 512 + hh * 64: gg * 512 + (hh + 1) * 64]
                        rhsC = HB[pb_:pb_ + 64, (gg * 8 + hh) * 512 + c8 * 64:
                                  (gg * 8 + hh) * 512 + c8 * 64 + 64]
                        self.mm(self.psum[2][0:64, hh * 64:(hh + 1) * 64], lhsP, rhsC,
                                True, True, "tq" if hh == 7 else None)
                    tqv = p.cnt["tq"]
                    yd = self.scrb[cb:cb + 64, 1600:2112]
                    srcY = self.m_yd2[pb_:pb_ + 64, T * 16384 + gg * 4096: T * 16384 + (gg + 1) * 4096] \
                        .rearrange("n (h l) -> n h l", h=8)[:, :, c8 * 64:(c8 + 1) * 64]
                    p.wait("sync", "ss")
                    p.wait("sync", "st0")
                    p.dma("sync", yd.rearrange("n (h l) -> n h l", h=8), srcY, "lw3")
                    p.wait("vector", "lw3")
                    p.wait("vector", "tq", tqv)
                    p.wait("vector", "st0")
                    p.op("vector", lambda e, yd=yd: e.tensor_add(
                        yd, yd, self.psum[2][0:64, 0:512]), "vs")
                    p.wait("sync", "vs")
                    p.dma("sync", srcY, yd.rearrange("n (h l) -> n h l", h=8), "st0")
                    p.wait("scalar", "vs")
                    p.op("scalar", lambda e: e.activation(
                        self.stat[0:1, 1024:1025], self.stat[0:1, 1024:1025], AF.Copy), "ss")
            p.wait("sync", "st0")
            for fc in range(32):
                par = fc % 2
                r = fc % 8
                q, gg = r // 4, fc // 8
                ysl = HB[:, 16384 + fc * 512: 16384 + (fc + 1) * 512]
                zsl = self.scrb[:, par * 512:(par + 1) * 512]
                base = T * 16384 + gg * 4096 + (2 * (r % 4)) * 512
                p.wait("sync", "vs")
                p.dma("sync", ysl[0:64, :],
                      self.m_yd2[64 * q: 64 * q + 64, base: base + 512], "lw3")
                p.dma("sync", ysl[64:128, :],
                      self.m_yd2[64 * q: 64 * q + 64, base + 512: base + 1024], "lw3")
                p.dma("sync", zsl, self.m_z[fc * 128:(fc + 1) * 128, T * 512:(T + 1) * 512], "lw0")
                p.wait("vector", "lw3")
                p.wait("vector", "lw0")
                p.op("vector", lambda e, ysl=ysl, zsl=zsl: e.tensor_mul(ysl, ysl, zsl), "vs")
            ubuf = HB[:, 16384:32768]
            self.emit_rms_inplace(ubuf, 32, self.g_nw, 0)
            wg = {}
            p.wait("gpsimd", "lg0")
            p.wait("gpsimd", "lg1")
            for f in range(NCH):
                par = f % 2
                wi = self.wt[:, par * 4096:(par + 1) * 4096]
                lsem = "lg0" if par == 0 else "lg1"
                if f >= 2:
                    p.wait("gpsimd", "vs", wg[f - 2])
                p.dma("gpsimd", wi[:, 0:4096].rearrange("p (k o) -> p k o", o=128),
                      self.w_out[:, f * 128:(f + 1) * 128].rearrange("(k p) o -> p k o", p=128),
                      lsem)
                ps = self.psum[4 + par][:, 0:512]
                p.wait("tensor", lsem)
                if f >= 2:
                    p.wait("tensor", "vs", wg[f - 2])
                for k in range(32):
                    self.mm(ps, wi[:, k * 128:(k + 1) * 128],
                            ubuf[:, k * 512:(k + 1) * 512], k == 0, k == 31,
                            "ts" if k == 31 else None)
                lnx = self.scrb[:, par * 512:(par + 1) * 512]
                p.wait("sync", "ts")
                p.dma("sync", lnx, self.m_lnx[f * 128:(f + 1) * 128, T * 512:(T + 1) * 512], "lw0")
                p.wait("vector", "lw0")
                p.wait("vector", "ts")
                inn = self.scrb[:, 1088 + par * 512: 1088 + (par + 1) * 512]
                p.wait("vector", "st1")
                p.op("vector", lambda e, inn=inn, ps=ps, lnx=lnx: e.tensor_add(inn, ps, lnx), "vs")
                wg[f] = p.cnt["vs"]
                p.wait("sync", "vs")
                p.dma("sync", self.m_mo[f * 128:(f + 1) * 128, T * 512:(T + 1) * 512], inn, "st1")

    def emit_mamba_mlp(self):
        p = self.p
        HB = self.hbuf
        p.wait("gpsimd", "ts")
        p.wait("gpsimd", "tq")
        for T in range(NTT):
            p.wait("sync", "st1")
            p.wait("sync", "ts")
            p.wait("sync", "vs")
            p.dma("sync", self.xln[:, :].rearrange("p (c t) -> p c t", t=TT),
                  self.m_mo[:, T * TT:(T + 1) * TT].rearrange("(c p) t -> p c t", p=128), "lw0")
            p.wait("vector", "lw0")
            p.wait("vector", "ss")
            self.emit_rms_inplace(self.xln, NCH, self.g_ln, NCH)
            mg = {}
            p.wait("gpsimd", "lg0")
            p.wait("gpsimd", "lg1")
            for i in range(44):
                par = i % 2
                wgt = self.wt[:, par * 2048:(par + 1) * 2048]
                wut = self.wt[:, 4096 + par * 2048: 4096 + (par + 1) * 2048]
                lg = "lg0" if par == 0 else "lg1"
                if i >= 2:
                    p.wait("gpsimd", "vs", mg[i - 2])
                p.dma("gpsimd", wgt.rearrange("p (k o) -> p k o", o=128),
                      self.mgw[:, i * 128:(i + 1) * 128].rearrange("(k p) o -> p k o", p=128), lg)
                p.dma("gpsimd", wut.rearrange("p (k o) -> p k o", o=128),
                      self.muw[:, i * 128:(i + 1) * 128].rearrange("(k p) o -> p k o", p=128), lg)
                ps_g = self.psum[par][:, 0:512]
                ps_u = self.psum[2 + par][:, 0:512]
                p.wait("tensor", lg)
                if i >= 2:
                    p.wait("tensor", "vs", mg[i - 2])
                    p.wait("tensor", "ss")
                for k in range(NCH):
                    self.mm(ps_g, wgt[:, k * 128:(k + 1) * 128],
                            self.xln[:, k * TT:(k + 1) * TT], k == 0, k == NCH - 1)
                for k in range(NCH):
                    self.mm(ps_u, wut[:, k * 128:(k + 1) * 128],
                            self.xln[:, k * TT:(k + 1) * TT], k == 0, k == NCH - 1,
                            "ts" if k == NCH - 1 else None)
                sg = self.scrb[:, par * 512:(par + 1) * 512]
                ht = HB[:, i * 512:(i + 1) * 512]
                p.wait("scalar", "ts")
                p.op("scalar", lambda e, sg=sg, ps_g=ps_g: e.activation(sg, ps_g, AF.Silu), "ss")
                p.wait("vector", "ss")
                p.op("vector", lambda e, ht=ht, ps_u=ps_u, sg=sg: e.tensor_mul(ht, ps_u, sg), "vs")
                mg[i] = p.cnt["vs"]
            p.wait("tensor", "vs")
            for fg in range(4):
                tg = {}
                p.wait("gpsimd", "ts")
                p.wait("gpsimd", "lg0")
                p.wait("gpsimd", "lg1")
                for i in range(44):
                    par = i % 2
                    wd = self.wt[:, 8192 + par * 512: 8192 + (par + 1) * 512]
                    lg = "lg0" if par == 0 else "lg1"
                    if i >= 2:
                        p.wait("gpsimd", "ts", tg[i - 2])
                    p.dma("gpsimd", wd,
                          self.mdw[i * 128:(i + 1) * 128, fg * 512:(fg + 1) * 512], lg)
                    p.wait("tensor", lg)
                    if i == 0:
                        p.wait("tensor", "vs")
                    for j in range(4):
                        self.mm(self.psum[j][:, 0:512],
                                wd[:, j * 128:(j + 1) * 128],
                                HB[:, i * 512:(i + 1) * 512],
                                i == 0, i == 43, "ts" if j == 3 else None)
                    tg[i] = p.cnt["ts"]
                p.wait("vector", "ts")
                for j in range(4):
                    f = fg * 4 + j
                    par = j % 2
                    inn = self.scrb[:, par * 512:(par + 1) * 512]
                    p.wait("sync", "vs")
                    p.dma("sync", inn,
                          self.m_mo[f * 128:(f + 1) * 128, T * TT:(T + 1) * TT], "lw0")
                    p.wait("vector", "lw0")
                    tmp = self.scrb[:, 1088 + par * 512: 1088 + (par + 1) * 512]
                    p.wait("vector", "vs")
                    p.op("vector", lambda e, tmp=tmp, j=j, inn=inn: e.tensor_add(
                        tmp, self.psum[j][:, 0:512], inn), "vs")
                    dst = self.xres[:, f * TOK + T * TT: f * TOK + T * TT + TT]
                    p.wait("vector", "vs")
                    p.op("vector", lambda e, dst=dst, tmp=tmp: e.tensor_add(dst, dst, tmp), "vs")
                p.wait("tensor", "vs")

    def finalize(self):
        block = self.es.enter_context(self.nc.Block())
        self.p.finalize(block)
        self.es.__exit__(None, None, None)
        return self.nc


def build_nc(parts=("all",)):
    mk = MK()
    mk.emit_smalls()
    mk.emit_input()
    P_ = mk.p
    full = "all" in parts

    def prep_ffn0():
        mk.emit_prep(mk.g0w, mk.st_g0, mk.g0T, 8192, D)
        mk.emit_prep(mk.u0w, mk.st_u0, mk.u0T, 8192, D)
        mk.emit_prep(mk.d0w, mk.st_d0, mk.d0T, D, 8192)
        P_.wait("sync", "sp0")
        P_.wait("sync", "sp1")

    def prep_attn():
        mk.at_tq = {}
        mk.at_vss = {}
        mk.at_ss_gate = {}
        cds = [mk.emit_cast(mk.wq_h, mk.st_qkv[0:256, :], 256, D),
               mk.emit_cast(mk.wk_h, mk.st_qkv[256:512, :], 256, D),
               mk.emit_cast(mk.wv_h, mk.st_qkv[512:768, :], 256, D)]
        mk.emit_transpose_prep(mk.st_qkv, mk.qkvT, 768, D, cds)
        mk.emit_prep(mk.wo, mk.st_wo, mk.woT, D, D)
        P_.wait("sync", "sp0")
        P_.wait("sync", "sp1")
        P_.wait("sync", "ts")

    def prep_ffn1():
        mk.emit_prep(mk.f1w0, mk.st_f0, mk.f1w0T, 8192, D)
        mk.emit_prep(mk.f1w1, mk.st_f1, mk.f1w1T, D, 8192)
        P_.wait("sync", "sp0")
        P_.wait("sync", "sp1")
        P_.wait("sync", "ts")

    if full:
        prep_ffn0()
        mk.emit_ffn_quant()
        prep_attn()
        mk.emit_attn()
        P_.wait("sync", "ts")
        P_.wait("sync", "tq")
        mk.emit_ffn_quant()
        prep_ffn1()
        mk.emit_ffn_gelu()
        mk.emit_mamba()
        P_.wait("sync", "ts")
        P_.wait("sync", "tq")
        mk.emit_ffn_gelu()
    else:
        if "ffn0" in parts:
            prep_ffn0()
            mk.emit_ffn_quant()
        if "attn" in parts:
            prep_attn()
            mk.emit_attn()
        if "ffn1" in parts:
            prep_ffn1()
            mk.emit_ffn_gelu()
        if "mamba" in parts:
            mk.emit_mamba()
    mk.emit_output()
    return mk.finalize()


def make_inmaps(inputs, parts=("all",)):
    bf = ml_dtypes.bfloat16
    x = np.asarray(inputs["L0_attn_x"]).reshape(SEQ, D)
    Wq = np.asarray(inputs["L0_attn_Wq"])
    Wk = np.asarray(inputs["L0_attn_Wk"])
    Wv = np.asarray(inputs["L0_attn_Wv"])
    in_maps = []
    for core in range(NCORE):
        m = {
            "xin": np.ascontiguousarray(x[core * TOK:(core + 1) * TOK]),
            "wq_h": np.ascontiguousarray(Wq[core * 256:(core + 1) * 256]),
            "wk_h": np.ascontiguousarray(Wk[core * 256:(core + 1) * 256]),
            "wv_h": np.ascontiguousarray(Wv[core * 256:(core + 1) * 256]),
            "wo": np.asarray(inputs["L0_attn_Wo"]),
            "g0w": np.asarray(inputs["L0_ffn_gate_w"], dtype=np.int32),
            "u0w": np.asarray(inputs["L0_ffn_up_w"], dtype=np.int32),
            "d0w": np.asarray(inputs["L0_ffn_down_w"], dtype=np.int32),
            "g0s": np.asarray(inputs["L0_ffn_gate_scale"]),
            "u0s": np.asarray(inputs["L0_ffn_up_scale"]),
            "d0s": np.asarray(inputs["L0_ffn_down_scale"]),
            "w_in": np.asarray(inputs["L1_attn_w_in"]),
            "conv_w": np.asarray(inputs["L1_attn_conv_w"]),
            "conv_b": np.asarray(inputs["L1_attn_conv_b"]),
            "dt_bias": np.asarray(inputs["L1_attn_dt_bias"]),
            "a_log": np.asarray(inputs["L1_attn_A_log"]),
            "d_vec": np.asarray(inputs["L1_attn_D"]),
            "norm_w": np.asarray(inputs["L1_attn_norm_w"]),
            "w_out": np.asarray(inputs["L1_attn_w_out"]),
            "ln1_w": np.asarray(inputs["L1_attn_ln1_w"]),
            "ln2_w": np.asarray(inputs["L1_attn_ln2_w"]),
            "mgw": np.asarray(inputs["L1_attn_gate_w"]),
            "muw": np.asarray(inputs["L1_attn_up_w"]),
            "mdw": np.asarray(inputs["L1_attn_down_w"]),
            "f1w0": np.asarray(inputs["L1_ffn_W0"]),
            "f1b0": np.asarray(inputs["L1_ffn_bias0"]),
            "f1w1": np.asarray(inputs["L1_ffn_W1"]),
            "f1b1": np.asarray(inputs["L1_ffn_bias1"]),
        }
        m.update(host_constants(core))
        in_maps.append(m)
    return in_maps


def kernel(**inputs):
    nc = build_nc(("all",))
    in_maps = make_inmaps(inputs)
    res = run_bass_kernel_spmd(nc, in_maps, core_ids=list(range(NCORE)))
    y = np.concatenate([res.results[i]["yout"] for i in range(NCORE)], axis=0)
    return y.reshape(1, SEQ, D).astype(np.float32)


# version 03 (final)
# speedup vs baseline: 1.0091x; 1.0091x over previous
import contextlib
import numpy as np
import ml_dtypes
import concourse.bass as bass
import concourse.mybir as mybir
from concourse.bass_utils import run_bass_kernel_spmd

F32 = mybir.dt.float32
BF16 = mybir.dt.bfloat16
I32 = mybir.dt.int32
AF = mybir.ActivationFunctionType
ALU = mybir.AluOpType

NCORE = 8
SEQ = 8192
TOK = SEQ // NCORE
D = 2048
NCH = D // 128
TT = 512
NTT = TOK // TT
ENGINES = ("tensor", "vector", "scalar", "gpsimd", "sync")

D_INNER = 4096
D_XB = 512
N_HEADS = 64
D_STATE = 64
CHUNK = 64
N_CHUNK = TOK // CHUNK
INTER = 5632
CONV_DIM = 2 * D_XB + D_INNER
ZXD = D_INNER + CONV_DIM + N_HEADS


class P:

    def __init__(self, nc, es):
        self.nc = nc
        self.es = es
        self.ops = {e: [] for e in ENGINES}
        self.cnt = {}
        self.semh = {}

    def sem(self, name):
        self.semh[name] = self.es.enter_context(self.nc.semaphore(name))
        self.cnt[name] = 0
        return name

    def on(self, eng, fn):
        self.ops[eng].append(fn)

    def wait(self, eng, name, val=None):
        if val is None:
            val = self.cnt[name]
        assert val <= self.cnt[name], ("wait-ahead!", eng, name, val, self.cnt[name])
        if val <= 0:
            return
        h = self.semh[name]
        self.on(eng, lambda e, h=h, v=val: e.wait_ge(h, v))

    def op(self, eng, f, sem=None, amt=1):
        if sem is None:
            self.on(eng, f)
            return None
        self.cnt[sem] += amt
        h = self.semh[sem]
        self.on(eng, lambda e, f=f, h=h, a=amt: f(e).then_inc(h, a))
        return self.cnt[sem]

    def dma(self, q, out, in_, sem=None, slow=False):
        return self.op(
            q, lambda e: e.dma_start(out=out, in_=in_,
                                     allow_slow_non_contiguous=slow), sem, 16)

    def finalize(self, block):
        for eng in ENGINES:
            ops = self.ops[eng]
            if not ops:
                continue

            def runner(e, ops=ops):
                for f in ops:
                    f(e)

            getattr(block, eng)(runner)


def host_constants(core):
    c = {}
    theta = (10000.0 ** (-(np.arange(64, dtype=np.float32) * 2.0) / 128.0)).astype(np.float32)
    pos = np.arange(SEQ, dtype=np.float32)
    ang = pos[None, :] * theta[:, None]
    c["c_cos"] = np.cos(ang).astype(np.float32)
    c["c_sin"] = np.sin(ang).astype(np.float32)
    c["c_eye128"] = np.eye(128, dtype=np.float32)
    tl = np.tril(np.ones((64, 64), np.float32)).T
    c["c_tril64"] = np.concatenate([tl, tl], axis=0).astype(ml_dtypes.bfloat16)
    c["c_eye128b"] = np.eye(128, dtype=ml_dtypes.bfloat16)
    ki = np.arange(128)[:, None]
    qi = np.arange(896)[None, :] - 384
    c["c_cmask"] = (qi >= ki).astype(np.float32).astype(ml_dtypes.bfloat16)
    rm = (np.arange(NCORE) < core).astype(np.float32)
    pm = (np.arange(NCORE) == core - 1).astype(np.float32)
    c["c_rmask"] = np.tile(rm[None, :], (128, 1)).astype(np.float32)
    c["c_rinv"] = (1.0 - c["c_rmask"]).astype(np.float32)
    c["c_pmask"] = np.tile(pm[None, :], (128, 1)).astype(np.float32)
    return c


SHARED_SPACE = "Shared"

class MK:
    def __init__(self):
        self.nc = bass.Bass("TRN2", target_bir_lowering=False, debug=False)
        nc = self.nc
        ein = lambda n, s, d: nc.dram_tensor(n, s, d, kind="ExternalInput").ap()
        self.xin = ein("xin", [TOK, D], F32)
        self.wq_h = ein("wq_h", [256, D], F32)
        self.wk_h = ein("wk_h", [256, D], F32)
        self.wv_h = ein("wv_h", [256, D], F32)
        self.wo = ein("wo", [D, D], F32)
        self.g0w = ein("g0w", [8192, D], I32)
        self.u0w = ein("u0w", [8192, D], I32)
        self.d0w = ein("d0w", [D, 8192], I32)
        self.g0s = ein("g0s", [8192], F32)
        self.u0s = ein("u0s", [8192], F32)
        self.d0s = ein("d0s", [D], F32)
        self.w_in = ein("w_in", [D, ZXD], F32)
        self.conv_w = ein("conv_w", [CONV_DIM, 4], F32)
        self.conv_b = ein("conv_b", [CONV_DIM], F32)
        self.dt_bias = ein("dt_bias", [N_HEADS], F32)
        self.a_log = ein("a_log", [N_HEADS], F32)
        self.d_vec = ein("d_vec", [N_HEADS], F32)
        self.norm_w = ein("norm_w", [D_INNER], F32)
        self.w_out = ein("w_out", [D_INNER, D], F32)
        self.ln1_w = ein("ln1_w", [D], F32)
        self.ln2_w = ein("ln2_w", [D], F32)
        self.mgw = ein("mgw", [D, INTER], F32)
        self.muw = ein("muw", [D, INTER], F32)
        self.mdw = ein("mdw", [INTER, D], F32)
        self.f1w0 = ein("f1w0", [8192, D], F32)
        self.f1b0 = ein("f1b0", [8192], F32)
        self.f1w1 = ein("f1w1", [D, 8192], F32)
        self.f1b1 = ein("f1b1", [D], F32)
        self.c_cos = ein("c_cos", [64, SEQ], F32)
        self.c_sin = ein("c_sin", [64, SEQ], F32)
        self.c_eye128 = ein("c_eye128", [128, 128], F32)
        self.c_tril64 = ein("c_tril64", [128, 64], BF16)
        self.c_eye128b = ein("c_eye128b", [128, 128], BF16)
        self.c_cmask = ein("c_cmask", [128, 896], BF16)
        self.c_rmask = ein("c_rmask", [128, 8], F32)
        self.c_rinv = ein("c_rinv", [128, 8], F32)
        self.c_pmask = ein("c_pmask", [128, 8], F32)

        self.yout = nc.dram_tensor("yout", [TOK, D], F32, kind="ExternalOutput").ap()

        dr = lambda n, s, d, **kw: nc.dram_tensor(n, s, d, kind="Internal", **kw).ap()
        self.st_g0 = dr("st_g0", [8192, D], BF16)
        self.st_u0 = dr("st_u0", [8192, D], BF16)
        self.st_d0 = dr("st_d0", [D, 8192], BF16)
        self.st_f0 = dr("st_f0", [8192, D], BF16)
        self.st_f1 = dr("st_f1", [D, 8192], BF16)
        self.st_qkv = dr("st_qkv", [768, D], BF16)
        self.st_wo = dr("st_wo", [D, D], BF16)
        self.g0T = dr("g0T", [D, 8192], BF16)
        self.u0T = dr("u0T", [D, 8192], BF16)
        self.d0T = dr("d0T", [8192, D], BF16)
        self.f1w0T = dr("f1w0T", [D, 8192], BF16)
        self.f1w1T = dr("f1w1T", [8192, D], BF16)
        self.qkvT = dr("qkvT", [D, 768], BF16)
        self.woT = dr("woT", [D, D], BF16)
        self.vsc = dr("vsc", [SEQ, 256], BF16)
        self.gb_xln_in = dr("gb_xln_in", [D, TOK], BF16)
        self.gb_xln_out = dr("gb_xln_out", [NCORE * D, TOK], BF16, addr_space=SHARED_SPACE)
        self.gb_ao_in = dr("gb_ao_in", [D, TOK], BF16)
        self.gb_ao_out = dr("gb_ao_out", [D, TOK], BF16)
        self.gb_halo_in = dr("gb_halo_in", [128, 160], BF16)
        self.gb_halo_out = dr("gb_halo_out", [NCORE * 128, 160], BF16, addr_space=SHARED_SPACE)
        self.gb_seg_in = dr("gb_seg_in", [128, 4096], F32)
        self.gb_seg_out = dr("gb_seg_out", [NCORE * 128, 4096], F32, addr_space=SHARED_SPACE)
        self.m_z = dr("m_z", [D_INNER, TOK], BF16)
        self.m_xbc = dr("m_xbc", [CONV_DIM, TOK], BF16)
        self.m_cod = dr("m_cod", [128, 32768], BF16)
        self.m_ydiag = dr("m_ydiag", [D_INNER, TOK], BF16)
        self.m_rows = dr("m_rows", [64, TOK], F32)
        self.m_rows2 = dr("m_rows2", [TOK, 64], F32)
        self.m_rows3 = dr("m_rows3", [64, 32], F32)
        self.m_cT = dr("m_cT", [D_INNER, TOK], BF16)
        self.m_rows2w = dr("m_rows2w", [TOK, 64], F32)
        self.m_lnx = dr("m_lnx", [D, TOK], BF16)
        self.m_acum = dr("m_acum", [64, TOK], F32)
        self.m_od = dr("m_od", [64, TOK], F32)
        self.m_yd2 = dr("m_yd2", [128, 32768], BF16)
        self.m_prev = dr("m_prev", [N_CHUNK * 128, 2048], F32)
        self.m_dbg = dr("m_dbg", [64, 4096], F32)
        self.m_mo = dr("m_mo", [D, TOK], BF16)

        es = self.es = contextlib.ExitStack()
        es.__enter__()
        self.p = P(nc, es)
        self._alloc()

    def _alloc(self):
        nc, es, p = self.nc, self.es, self.p
        sb = lambda n, s, d: es.enter_context(nc.sbuf_tensor(n, s, d))
        self.xres = sb("xres", [128, NCH * TOK], F32)
        self.xln = sb("xln", [128, NCH * TT], BF16)
        self.hbuf = sb("hbuf", [128, 32768], BF16)
        self.wt = sb("wt", [128, 13056], BF16)
        self.scr = sb("scr", [128, 3072], F32)
        self.scrb = sb("scrb", [128, 2176], BF16)
        self.prs = sb("prs", [128, 2816], BF16)
        self.ones_col = sb("ones_col", [128, 1], BF16)
        self.ones_row = sb("ones_row", [1, 512], BF16)
        self.ones_r128b = sb("ones_r128b", [1, 128], BF16)
        self.ones_r128f = sb("ones_r128f", [1, 128], F32)
        self.mrow = sb("mrow", [1, 128], BF16)
        self.stat = sb("stat", [1, 3 * TT], F32)
        self.statb = sb("statb", [1, 2 * TT // 2], BF16)
        self.sc_g = sb("sc_g", [128, 64], F32)
        self.sc_u = sb("sc_u", [128, 64], F32)
        self.sc_d = sb("sc_d", [128, 16], F32)
        self.b0pp = sb("b0pp", [128, 64], F32)
        self.konst = sb("konst", [128, 8], F32)
        self.cbpp = sb("cbpp", [128, 40], F32)
        self.b1row = sb("b1row", [1, D], BF16)
        self.g_ln = sb("g_ln", [128, 2 * NCH], F32)
        self.g_nw = sb("g_nw", [128, 36], F32)
        self.eye128 = sb("eye128", [128, 128], F32)
        self.tril64 = sb("tril64", [128, 64], BF16)
        self.eye128b = sb("eye128b", [128, 128], BF16)
        self.rmask = sb("rmask", [128, 8], F32)
        self.rinv = sb("rinv", [128, 8], F32)
        self.pmask = sb("pmask", [128, 8], F32)
        self.psum = [es.enter_context(nc.psum_tensor(f"ps{i}", [128, 512], F32))
                     for i in range(8)]
        for n in ["lw0", "lw1", "lw2", "lw3",
                  "lg0", "lg1",
                  "lp0", "lp1",
                  "sp0", "sp1",
                  "lx", "lsm", "lgs",
                  "st0", "st1", "st2", "stg",
                  "cc",
                  "ts", "vs", "ss", "tp", "tq"]:
            p.sem(n)

    def mm(self, out, lhsT, rhs, start, stop, sem=None):
        return self.p.op(
            "tensor",
            lambda e: e.matmul(out, lhsT, rhs, start=start, stop=stop,
                               skip_group_check=True),
            sem)

    def emit_smalls(self):
        p = self.p
        for dst, src in [(self.eye128[:, :], self.c_eye128),
                         (self.tril64[:, :], self.c_tril64),
                         (self.eye128b[:, :], self.c_eye128b),
                         (self.rmask[:, :], self.c_rmask),
                         (self.rinv[:, :], self.c_rinv),
                         (self.pmask[:, :], self.c_pmask)]:
            p.dma("sync", dst, src, "lsm")

        def vec_pp(dst, src):
            p.dma("sync", dst, src.rearrange("(c p) -> p c", p=128), "lsm", slow=True)

        vec_pp(self.sc_g[:, 0:64], self.g0s)
        vec_pp(self.sc_u[:, 0:64], self.u0s)
        vec_pp(self.sc_d[:, 0:16], self.d0s)
        vec_pp(self.g_ln[:, 0:NCH], self.ln1_w)
        vec_pp(self.g_ln[:, NCH:2 * NCH], self.ln2_w)
        vec_pp(self.g_nw[:, 0:32], self.norm_w)
        vec_pp(self.b0pp[:, 0:64], self.f1b0)
        p.dma("sync", self.cbpp[:, 0:40],
              self.conv_b.rearrange("(c p) -> p c", p=128), "lsm", slow=True)
        p.dma("sync", self.g_nw[0:64, 32:33], self.dt_bias.unsqueeze(1), "lsm", slow=True)
        p.dma("sync", self.g_nw[0:64, 33:34], self.a_log.unsqueeze(1), "lsm", slow=True)
        p.dma("gpsimd", self.b1row[0:1, :], self.f1b1.unsqueeze(0), "lgs")
        p.op("vector", lambda e: e.memset(self.ones_col[:, :], 1.0), "vs")
        p.op("vector", lambda e: e.memset(self.ones_row[:, :], 1.0), "vs")
        p.op("vector", lambda e: e.memset(self.ones_r128b[:, :], 1.0), "vs")
        p.op("vector", lambda e: e.memset(self.ones_r128f[:, :], 1.0), "vs")
        p.op("vector", lambda e: e.memset(self.mrow[:, :], -128.0), "vs")
        p.op("vector", lambda e: e.memset(self.konst[:, 0:1], 1e-5), "vs")
        p.op("vector", lambda e: e.memset(self.konst[:, 1:2], -0.5), "vs")
        p.op("vector", lambda e: e.memset(self.konst[:, 2:3], 1e-6), "vs")
        p.op("vector", lambda e: e.memset(self.konst[:, 3:4], -1.0), "vs")
        p.op("vector", lambda e: e.memset(self.konst[:, 4:5], 1.0), "vs")
        p.wait("scalar", "vs")
        p.wait("vector", "lsm")
        p.op("vector", lambda e: e.tensor_scalar_mul(
            self.sc_u[:, 0:64], self.sc_u[:, 0:64], 0.5), "vs")
        p.wait("vector", "lgs")
        p.wait("scalar", "lsm")

    def emit_input(self):
        p = self.p
        for tt in range(TOK // 128):
            buf = self.scr[:, 0:2048]
            if tt >= 1:
                p.wait("sync", "tp")
            p.dma("sync", buf, self.xin[tt * 128:(tt + 1) * 128, :], "lx")
            p.wait("tensor", "lx")
            for c in range(NCH):
                pb = self.psum[c % 2][:, 0:128]
                p.wait("tensor", "vs")
                p.op("tensor", lambda e, pb=pb, buf=buf, c=c: e.transpose(
                    pb, buf[:, c * 128:(c + 1) * 128], self.eye128[:, :]), "tp")
                p.wait("vector", "tp")
                dst = self.xres[:, c * TOK + tt * 128: c * TOK + (tt + 1) * 128]
                p.op("vector", lambda e, dst=dst, pb=pb: e.tensor_copy(dst, pb), "vs")

    def emit_output(self):
        p = self.p
        for e in ("tensor", "vector", "sync"):
            for s in ("ts", "vs", "ss", "tp", "tq"):
                p.wait(e, s)
        for tt in range(TOK // 128):
            buf = self.scr[:, 0:2048]
            stsem = "st0"
            if tt >= 1:
                p.wait("vector", stsem)
            for c in range(NCH):
                pb = self.psum[c % 2][:, 0:128]
                p.wait("tensor", "vs")
                src = self.xres[:, c * TOK + tt * 128: c * TOK + (tt + 1) * 128]
                p.op("tensor", lambda e, pb=pb, src=src: e.transpose(
                    pb, src, self.eye128[:, :]), "tp")
                p.wait("vector", "tp")
                p.op("vector", lambda e, buf=buf, c=c, pb=pb: e.tensor_copy(
                    buf[:, c * 128:(c + 1) * 128], pb), "vs")
            p.wait("sync", "vs")
            p.dma("sync", self.yout[tt * 128:(tt + 1) * 128, :], buf, stsem)
        p.wait("sync", "st0")
        p.wait("sync", "st1")

    _cast_rot = 0

    def emit_cast(self, src, dst, rows, cols):
        p = self.p
        sem = "lg0" if MK._cast_rot % 2 == 0 else "lg1"
        MK._cast_rot += 1
        p.wait("gpsimd", sem)
        RB = 4096
        for r0 in range(0, rows, RB):
            r1 = min(rows, r0 + RB)
            p.dma("gpsimd", dst[r0:r1, 0:cols], src[r0:r1, :], sem)
        return sem, p.cnt[sem]

    def emit_transpose_prep(self, stg, dstT, rows, cols, cast_done):
        p = self.p
        if isinstance(cast_done, tuple):
            cast_done = [cast_done]
        for s, v in cast_done:
            p.wait("scalar", s, v)
        RB = 1408
        tiles = [(b, r0) for b in range(cols // 128) for r0 in range(0, rows, RB)]
        n = len(tiles)
        lcnt = {}
        for idx in range(n + 1):
            if idx < n:
                b, r0 = tiles[idx]
                par = idx % 2
                half = self.prs[:, par * 1408:(par + 1) * 1408]
                lsem = "lp0" if par == 0 else "lp1"
                rl = min(RB, rows - r0)
                if idx >= 2:
                    p.wait("scalar", "sp0" if par == 0 else "sp1")
                p.op("scalar", lambda e, half=half, b=b, r0=r0, rl=rl, stg=stg: e.dma_start(
                    out=half[:, 0:rl], in_=stg[r0:r0 + rl, b * 128:(b + 1) * 128],
                    transpose=True), lsem, 16)
                lcnt[idx] = p.cnt[lsem]
            if idx >= 1:
                j = idx - 1
                b, r0 = tiles[j]
                par = j % 2
                half = self.prs[:, par * 1408:(par + 1) * 1408]
                rl = min(RB, rows - r0)
                p.wait("scalar", "lp0" if par == 0 else "lp1", lcnt[j])
                p.op("scalar", lambda e, half=half, b=b, r0=r0, rl=rl, dstT=dstT: e.dma_start(
                    out=dstT[b * 128:(b + 1) * 128, r0:r0 + rl], in_=half[:, 0:rl]),
                    "sp0" if par == 0 else "sp1", 16)
        p.wait("scalar", "sp0")
        p.wait("scalar", "sp1")
        return p.cnt["sp0"], p.cnt["sp1"]

    def emit_prep(self, src, stg, dstT, rows, cols):
        cd = self.emit_cast(src, stg, rows, cols)
        return self.emit_transpose_prep(stg, dstT, rows, cols, cd)

    def emit_layernorm(self, T, rms=False, gain_col=None, eps=1e-5):
        p = self.p
        ps_s = self.psum[6][0:1, 0:512]
        ps_q = self.psum[7][0:1, 0:512]
        ps_a = self.psum[6][:, 0:512]
        ps_b = self.psum[7][:, 0:512]
        p.wait("scalar", "vs")
        ln_gate = {}
        for c in range(NCH):
            par = c % 2
            src = self.xres[:, c * TOK + T * TT: c * TOK + T * TT + TT]
            xbf = self.scrb[:, par * 1024: par * 1024 + 512]
            xsq = self.scrb[:, par * 1024 + 512: par * 1024 + 1024]
            if c >= 2:
                p.wait("scalar", "ts", ln_gate[c - 2])
            if not rms:
                p.op("scalar", lambda e, xbf=xbf, src=src: e.activation(xbf, src, AF.Copy), "ss")
            p.op("scalar", lambda e, xsq=xsq, src=src: e.activation(xsq, src, AF.Square), "ss")
            p.wait("tensor", "ss")
            if not rms:
                self.mm(ps_s, self.ones_col[:, :], xbf, c == 0, c == NCH - 1)
            self.mm(ps_q, self.ones_col[:, :], xsq, c == 0, c == NCH - 1, "ts")
            ln_gate[c] = p.cnt["ts"]
        p.wait("vector", "ts")
        s_m = self.stat[0:1, 0:512]
        s_v = self.stat[0:1, 512:1024]
        s_b = self.stat[0:1, 1024:1536]
        s_t = s_b
        if not rms:
            p.op("vector", lambda e: e.tensor_scalar_mul(s_m, ps_s, 1.0 / D), "vs")
        p.op("vector", lambda e: e.tensor_scalar_mul(s_v, ps_q, 1.0 / D), "vs")
        if not rms:
            p.wait("vector", "vs")
            p.op("vector", lambda e: e.tensor_mul(s_t, s_m, s_m), "vs")
            p.wait("vector", "vs")
            p.op("vector", lambda e: e.tensor_sub(s_v, s_v, s_t), "vs")
        p.wait("scalar", "vs")
        ek = self.konst[0:1, 0:1] if eps == 1e-5 else self.konst[0:1, 2:3]
        p.op("scalar", lambda e, ek=ek: e.activation(s_v, s_v, AF.Ln, bias=ek), "ss")
        p.wait("scalar", "ss")
        p.op("scalar", lambda e: e.activation(s_v, s_v, AF.Exp, scale=self.konst[0:1, 1:2]), "ss")
        p.wait("vector", "ss")
        if not rms:
            p.op("vector", lambda e: e.tensor_mul(s_b, s_m, s_v), "vs")
        p.wait("tensor", "vs")
        p.wait("tensor", "ss")
        self.mm(ps_a, self.ones_r128f[:, :], s_v, True, True, "tq")
        if not rms:
            self.mm(ps_b, self.ones_r128f[:, :], s_b, True, True, "tq")
        p.wait("vector", "tq")
        for c in range(NCH):
            src = self.xres[:, c * TOK + T * TT: c * TOK + T * TT + TT]
            dst = self.xln[:, c * TT:(c + 1) * TT]
            if rms:
                g = self.g_ln[:, gain_col + c:gain_col + c + 1] if gain_col is not None else None
                p.op("vector", lambda e, dst=dst, src=src, g=g: e.scalar_tensor_tensor(
                    dst, src, g, ps_a, ALU.mult, ALU.mult), "vs")
            else:
                tmp = self.scr[:, 2048:2560]
                if c > 0:
                    p.wait("vector", "vs")
                p.op("vector", lambda e, tmp=tmp, src=src: e.tensor_mul(tmp, src, ps_a), "vs")
                p.wait("vector", "vs")
                p.op("vector", lambda e, dst=dst, tmp=tmp: e.tensor_sub(dst, tmp, ps_b), "vs")
        p.wait("tensor", "vs")

    def emit_ffn_quant(self):
        p = self.p
        for T in range(NTT):
            self.emit_layernorm(T)
            for s in ("lw0", "lw1", "lw2", "lw3"):
                self.p.wait("sync", s)
            ps_h = self.psum[4][0:1, 0:512]
            ss_gate = {}
            vs_gate = {}
            for i in range(64):
                par = i % 2
                wg = self.wt[:, par * 2048:(par + 1) * 2048]
                wu = self.wt[:, 4096 + par * 2048: 4096 + (par + 1) * 2048]
                lsemg = "lw0" if par == 0 else "lw1"
                lsemu = "lw2" if par == 0 else "lw3"
                if i >= 2:
                    p.wait("sync", "ss", ss_gate[i - 2])
                    p.wait("sync", "vs", vs_gate[i - 2])
                p.dma("sync", wg.rearrange("p (k o) -> p k o", o=128),
                      self.g0T[:, i * 128:(i + 1) * 128].rearrange("(k p) o -> p k o", p=128),
                      lsemg)
                p.dma("sync", wu.rearrange("p (k o) -> p k o", o=128),
                      self.u0T[:, i * 128:(i + 1) * 128].rearrange("(k p) o -> p k o", p=128),
                      lsemu)
                ps_g = self.psum[par][:, 0:512]
                ps_u = self.psum[2 + par][:, 0:512]
                p.wait("tensor", lsemg)
                p.wait("tensor", lsemu)
                if i >= 2:
                    p.wait("tensor", "ss", ss_gate[i - 2])
                    p.wait("tensor", "vs", vs_gate[i - 2])
                for k in range(NCH):
                    self.mm(ps_g, wg[:, k * 128:(k + 1) * 128],
                            self.xln[:, k * TT:(k + 1) * TT], k == 0, k == NCH - 1)
                for k in range(NCH):
                    self.mm(ps_u, wu[:, k * 128:(k + 1) * 128],
                            self.xln[:, k * TT:(k + 1) * TT], k == 0, k == NCH - 1,
                            "ts" if k == NCH - 1 else None)
                ht = self.hbuf[:, i * 512:(i + 1) * 512]
                sg = self.scrb[:, par * 512:(par + 1) * 512]
                p.wait("scalar", "ts")
                p.op("scalar", lambda e, sg=sg, ps_g=ps_g, i=i: e.activation(
                    sg, ps_g, AF.Silu, scale=self.sc_g[:, i:i + 1]), "ss")
                ss_gate[i] = p.cnt["ss"]
                p.wait("vector", "ss")
                p.op("vector", lambda e, ht=ht, ps_u=ps_u, sg=sg, i=i: e.scalar_tensor_tensor(
                    ht, ps_u, self.sc_u[:, i:i + 1], sg, ALU.mult, ALU.mult), "vs")
                vs_gate[i] = p.cnt["vs"]
                p.wait("tensor", "vs")
                self.mm(ps_h, self.ones_col[:, :], ht, i == 0, i == 63,
                        "ts" if i == 63 else None)
            p.wait("vector", "ts")
            hs = self.statb[0:1, 0:512]
            p.op("vector", lambda e, hs=hs: e.tensor_copy(hs, ps_h), "vs")
            p.wait("tensor", "vs")
            for s in ("lw0", "lw1"):
                p.wait("sync", s)
            for fg in range(4):
                ts_gate = {}
                for s in ("lw0", "lw1"):
                    p.wait("sync", s)
                p.wait("sync", "ts")
                for i in range(64):
                    par = i % 2
                    wd = self.wt[:, 8192 + par * 512: 8192 + (par + 1) * 512]
                    lsem = "lw0" if par == 0 else "lw1"
                    if i >= 2:
                        p.wait("sync", "ts", ts_gate[i - 2])
                    p.dma("sync", wd,
                          self.d0T[i * 128:(i + 1) * 128, fg * 512:(fg + 1) * 512],
                          lsem)
                    p.wait("tensor", lsem)
                    if i == 0:
                        p.wait("tensor", "vs")
                    for j in range(4):
                        self.mm(self.psum[j][:, 0:512],
                                wd[:, j * 128:(j + 1) * 128],
                                self.hbuf[:, i * 512:(i + 1) * 512],
                                i == 0, False, "ts" if j == 3 else None)
                    ts_gate[i] = p.cnt["ts"]
                for j in range(4):
                    self.mm(self.psum[j][:, 0:512], self.mrow[:, :], hs,
                            False, True, "ts" if j == 3 else None)
                p.wait("vector", "ts")
                for j in range(4):
                    f = fg * 4 + j
                    dst = self.xres[:, f * TOK + T * TT: f * TOK + T * TT + TT]
                    p.op("vector", lambda e, dst=dst, j=j, f=f: e.scalar_tensor_tensor(
                        dst, self.psum[j][:, 0:512], self.sc_d[:, f:f + 1], dst,
                        ALU.mult, ALU.add), "vs")
                p.wait("tensor", "vs")

    def emit_ffn_gelu(self):
        p = self.p
        for T in range(NTT):
            self.emit_layernorm(T)
            for s in ("lw0", "lw1"):
                self.p.wait("sync", s)
            ss_gate = {}
            for i in range(64):
                par = i % 2
                w0 = self.wt[:, par * 2048:(par + 1) * 2048]
                lsem = "lw0" if par == 0 else "lw1"
                if i >= 2:
                    p.wait("sync", "ss", ss_gate[i - 2])
                p.dma("sync", w0.rearrange("p (k o) -> p k o", o=128),
                      self.f1w0T[:, i * 128:(i + 1) * 128].rearrange("(k p) o -> p k o", p=128),
                      lsem)
                ps = self.psum[par][:, 0:512]
                p.wait("tensor", lsem)
                if i >= 2:
                    p.wait("tensor", "ss", ss_gate[i - 2])
                for k in range(NCH):
                    self.mm(ps, w0[:, k * 128:(k + 1) * 128],
                            self.xln[:, k * TT:(k + 1) * TT], k == 0, k == NCH - 1,
                            "ts" if k == NCH - 1 else None)
                ht = self.hbuf[:, i * 512:(i + 1) * 512]
                p.wait("scalar", "ts")
                p.op("scalar", lambda e, ht=ht, ps=ps, i=i: e.activation(
                    ht, ps, AF.Gelu, bias=self.b0pp[:, i:i + 1]), "ss")
                ss_gate[i] = p.cnt["ss"]
            p.wait("tensor", "ss")
            for fg in range(4):
                ts_gate = {}
                for s in ("lw0", "lw1"):
                    p.wait("sync", s)
                p.wait("sync", "ts")
                for i in range(64):
                    par = i % 2
                    w1 = self.wt[:, 8192 + par * 512: 8192 + (par + 1) * 512]
                    lsem = "lw0" if par == 0 else "lw1"
                    if i >= 2:
                        p.wait("sync", "ts", ts_gate[i - 2])
                    p.dma("sync", w1,
                          self.f1w1T[i * 128:(i + 1) * 128, fg * 512:(fg + 1) * 512],
                          lsem)
                    p.wait("tensor", lsem)
                    if i == 0:
                        p.wait("tensor", "vs")
                    for j in range(4):
                        self.mm(self.psum[j][:, 0:512],
                                w1[:, j * 128:(j + 1) * 128],
                                self.hbuf[:, i * 512:(i + 1) * 512],
                                i == 0, False, "ts" if j == 3 else None)
                    ts_gate[i] = p.cnt["ts"]
                for j in range(4):
                    f = fg * 4 + j
                    self.mm(self.psum[j][:, 0:512],
                            self.b1row[0:1, f * 128:(f + 1) * 128],
                            self.ones_row[:, :], False, True,
                            "ts" if j == 3 else None)
                p.wait("vector", "ts")
                for j in range(4):
                    f = fg * 4 + j
                    dst = self.xres[:, f * TOK + T * TT: f * TOK + T * TT + TT]
                    p.op("vector", lambda e, dst=dst, j=j: e.scalar_tensor_tensor(
                        dst, self.psum[j][:, 0:512], 0.5, dst,
                        ALU.mult, ALU.add), "vs")
                p.wait("tensor", "vs")


    def emit_collective(self, kind, in_ap, out_ap, alu="bypass"):
        p = self.p
        from concourse.aot_env import allow_ici_all_to_all

        def f(e):
            with allow_ici_all_to_all():
                return e.collective_compute(
                    kind, getattr(ALU, alu), replica_groups=[list(range(NCORE))],
                    ins=[in_ap], outs=[out_ap])
        return p.op("gpsimd", f, "cc", 1)

    def emit_attn(self):
        p = self.p
        self.cmask = self.prs[:, 1920:2816]
        p.wait("sync", "sp0")
        p.wait("sync", "sp1")
        p.dma("sync", self.cmask, self.c_cmask, "lsm")
        p.wait("vector", "lsm")
        H0 = 0
        Q_ = [0, 8192]
        K_ = [16384, 24576]
        for T in range(NTT):
            if T > 0:
                p.wait("vector", "st0")
            self.emit_layernorm(T)
            p.wait("sync", "vs")
            p.dma("sync",
                  self.gb_xln_in[:, T * TT:(T + 1) * TT].rearrange("(c p) t -> p c t", p=128),
                  self.xln[:, :].rearrange("p (c t) -> p c t", t=TT), "st0")
        p.wait("gpsimd", "st0")
        cc_x = self.emit_collective("AllGather", self.gb_xln_in.opt(), self.gb_xln_out.opt())
        p.wait("sync", "sp0")
        p.wait("sync", "sp1")
        p.dma("sync", self.wt[:, 0:12288].rearrange("p (k o) -> p k o", o=768),
              self.qkvT.rearrange("(k p) o -> p k o", p=128), "lw0")
        qkv_ld = p.cnt["lw0"]
        vs_gate = {}
        ts_gate = {}
        st_par = ["st0", "st1"]
        for tg in range(16):
            r, tl = tg // 2, (tg % 2) * TT
            p.wait("sync", "cc", cc_x)
            if tg >= 1:
                p.wait("sync", "ts", ts_gate[tg - 1])
            p.dma("sync", self.xln[:, :].rearrange("p (c t) -> p c t", t=TT),
                  self.gb_xln_out[2048 * r:2048 * (r + 1), tl:tl + TT]
                  .rearrange("(c p) t -> p c t", p=128), "lw1")
            if tg >= 1:
                p.wait("sync", "vs", vs_gate[tg - 1])
            p.dma("sync", self.scr[0:64, 1024:1536],
                  self.c_cos[:, tg * TT:(tg + 1) * TT], "lx")
            p.dma("sync", self.scr[0:64, 1536:2048],
                  self.c_sin[:, tg * TT:(tg + 1) * TT], "lx")
            p.wait("tensor", "lw0", qkv_ld)
            p.wait("tensor", "lw1")
            if tg >= 1:
                p.wait("tensor", "vs", vs_gate[tg - 1])
            for m in range(2):
                for h in range(2):
                    ps = self.psum[m * 2 + h][:, 0:512]
                    for kk in range(NCH):
                        self.mm(ps, self.wt[:, kk * 768 + m * 256 + h * 128:
                                            kk * 768 + m * 256 + (h + 1) * 128],
                                self.xln[:, kk * TT:(kk + 1) * TT],
                                kk == 0, kk == NCH - 1,
                                "ts" if (kk == NCH - 1) else None)
            qk_ts = p.cnt["ts"]
            for vt in range(4):
                ps = self.psum[4 + vt % 2][:, 0:256]
                pvt = (tg, vt - 2) if vt >= 2 else (tg - 1, vt + 2)
                if pvt in self.at_vss:
                    p.wait("tensor", "ss", self.at_vss[pvt])
                for kk in range(NCH):
                    self.mm(ps, self.xln[:, kk * TT + vt * 128: kk * TT + (vt + 1) * 128],
                            self.wt[:, kk * 768 + 512: kk * 768 + 768],
                            kk == 0, kk == NCH - 1,
                            "ts" if kk == NCH - 1 else None)
                sdst = self.scrb[:, (vt % 2) * 256:(vt % 2) * 256 + 256]
                p.wait("scalar", "ts")
                if tg > 0 or vt >= 2:
                    p.wait("scalar", st_par[vt % 2])
                p.op("scalar", lambda e, sdst=sdst, ps=ps: e.activation(sdst, ps, AF.Copy), "ss")
                self.at_vss[(tg, vt)] = p.cnt["ss"]
                p.wait("sync", "ss")
                p.dma("sync", self.vsc[tg * TT + vt * 128: tg * TT + (vt + 1) * 128, :],
                      sdst, st_par[vt % 2])
            ts_gate[tg] = p.cnt["ts"]
            p.wait("vector", "ts", qk_ts)
            p.wait("vector", "lx")
            cos = self.scr[0:64, 1024:1536]
            sin = self.scr[0:64, 1536:2048]
            for m in range(2):
                for h in range(2):
                    ps = self.psum[m * 2 + h]
                    dst = self.hbuf[:, (Q_[h] if m == 0 else K_[h]) + tg * TT:
                                    (Q_[h] if m == 0 else K_[h]) + (tg + 1) * TT]
                    a = self.scr[0:64, 0:512]
                    b = self.scr[0:64, 512:1024]
                    lo, hi = ps[0:64, 0:512], ps[64:128, 0:512]
                    if m + h > 0:
                        p.wait("vector", "vs")
                    p.op("vector", lambda e, a=a, lo=lo, cos=cos: e.tensor_mul(a, lo, cos), "vs")
                    p.op("vector", lambda e, b=b, hi=hi, sin=sin: e.tensor_mul(b, hi, sin), "vs")
                    p.wait("vector", "vs")
                    p.op("vector", lambda e, dst=dst, a=a, b=b: e.tensor_sub(dst[0:64, :], a, b), "vs")
                    p.wait("vector", "vs")
                    p.op("vector", lambda e, a=a, lo=lo, sin=sin: e.tensor_mul(a, lo, sin), "vs")
                    p.op("vector", lambda e, b=b, hi=hi, cos=cos: e.tensor_mul(b, hi, cos), "vs")
                    p.wait("vector", "vs")
                    p.op("vector", lambda e, dst=dst, a=a, b=b: e.tensor_add(dst[64:128, :], a, b), "vs")
            vs_gate[tg] = p.cnt["vs"]
            self.at_ss_gate[tg] = p.cnt["ss"]
        p.wait("sync", "st0")
        p.wait("sync", "st1")
        p.wait("sync", "lw2")
        p.wait("sync", "lw3")
        p.wait("tensor", "ss")
        p.wait("tensor", "vs")
        lsems = ["lw2", "lw3"]
        for h in range(2):
            for qt in range(16):
                nkt = 4 * qt + 4
                ps_o = self.psum[4][:, 0:512]
                ps_d = self.psum[6][0:1, 0:512]
                ps_r = self.psum[7][:, 0:512]
                vinfo = {}
                for kt in range(nkt + 1):
                    if kt < nkt:
                        par = kt % 2
                        ps_s = self.psum[par][:, 0:512]
                        s2 = self.scr[:, par * 512:(par + 1) * 512]
                        s4 = self.xln[:, par * 512:(par + 1) * 512]
                        vslot = self.wt[:, 12288 + par * 128: 12288 + (par + 1) * 128]
                        if self.at_tq.get("last_" + str(par)) is not None:
                            p.wait("sync", "tq", self.at_tq["last_" + str(par)])
                        p.dma("sync", vslot, self.vsc[kt * 128:(kt + 1) * 128,
                                                      h * 128:(h + 1) * 128], lsems[par])
                        vinfo[kt] = (ps_s, s2, s4, vslot, p.cnt[lsems[par]])
                        if self.at_tq.get("lastq_" + str(par)) is not None:
                            p.wait("tensor", "ss", self.at_tq["lastq_" + str(par)])
                        self.mm(ps_s, self.hbuf[:, K_[h] + kt * 128: K_[h] + (kt + 1) * 128],
                                self.hbuf[:, Q_[h] + qt * TT: Q_[h] + (qt + 1) * TT],
                                True, True, "ts")
                        tsv = p.cnt["ts"]
                        p.wait("scalar", "ts", tsv)
                        if self.at_tq.get("lastv_" + str(par)) is not None:
                            p.wait("scalar", "vs", self.at_tq["lastv_" + str(par)])
                        p.op("scalar", lambda e, s2=s2, ps_s=ps_s: e.activation(s2, ps_s, AF.Square), "ss")
                        ssv = p.cnt["ss"]
                        self.at_tq["lastq_" + str(par)] = ssv
                        p.wait("vector", "ss", ssv)
                        if self.at_tq.get("lasttq_" + str(par)) is not None:
                            p.wait("vector", "tq", self.at_tq["lasttq_" + str(par)])
                        p.op("vector", lambda e, s4=s4, s2=s2: e.tensor_mul(s4, s2, s2), "vs")
                        if kt >= 4 * qt:
                            moff = kt - 4 * qt
                            msk = self.cmask[:, 384 - 128 * moff: 896 - 128 * moff]
                            p.wait("vector", "vs")
                            p.op("vector", lambda e, s4=s4, msk=msk: e.tensor_mul(s4, s4, msk), "vs")
                        self.at_tq["lastv_" + str(par)] = p.cnt["vs"]
                        self.at_tq["vsv_" + str(kt % 2)] = p.cnt["vs"]
                    if kt >= 1:
                        j = kt - 1
                        parj = j % 2
                        _, _, s4j, vslotj, vl = vinfo[j]
                        p.wait("tensor", "vs", self.at_tq["vsv_" + str(parj)])
                        p.wait("tensor", lsems[parj], vl)
                        self.mm(ps_o, vslotj, s4j, j == 0, j == nkt - 1)
                        self.mm(ps_d, self.ones_col[:, :], s4j, j == 0, j == nkt - 1, "tq")
                        self.at_tq["last_" + str(parj)] = p.cnt["tq"]
                        self.at_tq["lasttq_" + str(parj)] = p.cnt["tq"]
                p.wait("vector", "tq")
                p.wait("vector", "st0")
                r1 = self.stat[0:1, 0:512]
                r2 = self.stat[0:1, 512:1024]
                p.op("vector", lambda e, r1=r1, ps_d=ps_d: e.tensor_scalar_max(r1, ps_d, 1e-6), "vs")
                p.wait("vector", "vs")
                p.op("vector", lambda e, r2=r2, r1=r1: e.reciprocal(r2, r1), "vs")
                p.wait("tensor", "vs")
                self.mm(ps_r, self.ones_r128f[:, :], r2, True, True, "tq")
                p.wait("vector", "tq")
                rbc = self.scr[:, 2048:2560]
                p.op("vector", lambda e, rbc=rbc, ps_r=ps_r: e.tensor_copy(rbc, ps_r), "vs")
                p.wait("vector", "vs")
                obf = self.scrb[:, 1024:1536]
                p.op("vector", lambda e, obf=obf, ps_o=ps_o, rbc=rbc: e.tensor_mul(obf, ps_o, rbc), "vs")
                p.wait("sync", "vs")
                j, tl = qt // 2, (qt % 2) * TT
                p.dma("sync", self.gb_ao_in[j * 256 + h * 128: j * 256 + (h + 1) * 128,
                                            tl:tl + TT], obf, "st0")
                p.wait("tensor", "vs")
        p.wait("gpsimd", "st0")
        cc_a = self.emit_collective("AllToAll", self.gb_ao_in.opt(), self.gb_ao_out.opt())
        for T in range(NTT):
            p.wait("sync", "cc", cc_a)
            p.wait("sync", "tq")
            p.dma("sync", self.xln[:, :].rearrange("p (c t) -> p c t", t=TT),
                  self.gb_ao_out[:, T * TT:(T + 1) * TT]
                  .rearrange("(c p) t -> p c t", p=128), "lw1")
            for fg in range(4):
                for s in ("lw0",):
                    p.wait("sync", s)
                p.wait("sync", "tq")
                p.dma("sync", self.wt[:, 0:8192].rearrange("p (k o) -> p k o", o=512),
                      self.woT[:, fg * 512:(fg + 1) * 512].rearrange("(k p) o -> p k o", p=128),
                      "lw0")
                p.wait("tensor", "lw0")
                p.wait("tensor", "lw1")
                p.wait("tensor", "vs")
                for kk in range(NCH):
                    for j in range(4):
                        self.mm(self.psum[j][:, 0:512],
                                self.wt[:, kk * 512 + j * 128: kk * 512 + (j + 1) * 128],
                                self.xln[:, kk * TT:(kk + 1) * TT],
                                kk == 0, kk == NCH - 1,
                                "tq" if (kk == NCH - 1 and j == 3) else None)
                p.wait("vector", "tq")
                for j in range(4):
                    f = fg * 4 + j
                    dst = self.xres[:, f * TOK + T * TT: f * TOK + T * TT + TT]
                    p.op("vector", lambda e, dst=dst, j=j: e.tensor_add(
                        dst, self.psum[j][:, 0:512], dst), "vs")
                p.wait("tensor", "vs")



    def emit_rms_inplace(self, buf, nch, gain, gcol0):
        p = self.p
        ps_q = self.psum[7][0:1, 0:512]
        ps_a = self.psum[7][:, 0:512]
        p.wait("scalar", "vs")
        for s_ in ("lw0", "lw3", "lg0", "lg1"):
            p.wait("scalar", s_)
        gate = {}
        for c in range(nch):
            par = c % 2
            t = buf[:, c * 512:(c + 1) * 512]
            xsq = self.scrb[:, par * 512:(par + 1) * 512]
            if c >= 2:
                p.wait("scalar", "ts", gate[c - 2])
            p.op("scalar", lambda e, xsq=xsq, t=t: e.activation(xsq, t, AF.Square), "ss")
            p.wait("tensor", "ss")
            self.mm(ps_q, self.ones_col[:, :], xsq, c == 0, c == nch - 1, "ts")
            gate[c] = p.cnt["ts"]
        p.wait("vector", "ts")
        s_v = self.stat[0:1, 512:1024]
        p.op("vector", lambda e, s_v=s_v, nch=nch: e.tensor_scalar_mul(
            s_v, ps_q, 1.0 / (nch * 128)), "vs")
        p.wait("scalar", "vs")
        p.op("scalar", lambda e, s_v=s_v: e.activation(
            s_v, s_v, AF.Ln, bias=self.konst[0:1, 0:1]), "ss")
        p.wait("scalar", "ss")
        p.op("scalar", lambda e, s_v=s_v: e.activation(
            s_v, s_v, AF.Exp, scale=self.konst[0:1, 1:2]), "ss")
        p.wait("tensor", "ss")
        self.mm(ps_a, self.ones_r128f[:, :], s_v, True, True, "tq")
        p.wait("vector", "tq")
        for c in range(nch):
            t = buf[:, c * 512:(c + 1) * 512]
            p.wait("vector", "vs")
            if gain is not None:
                g = gain[:, gcol0 + c:gcol0 + c + 1]
                p.op("vector", lambda e, t=t, g=g: e.scalar_tensor_tensor(
                    t, t, g, ps_a, ALU.mult, ALU.mult), "vs")
            else:
                p.op("vector", lambda e, t=t: e.tensor_mul(t, t, ps_a), "vs")
        p.wait("tensor", "vs")

    def emit_mamba(self):
        p = self.p
        xF = self.xln[:, :].bitcast(F32)
        prsF = self.prs[:, :].bitcast(F32)
        HB = self.hbuf
        p.wait("sync", "lsm")
        p.dma("sync", self.prs[:, :].bitcast(F32)[:, 0:160].rearrange("p (c w) -> p c w", w=4),
              self.conv_w.rearrange("(c p) w -> p c w", p=128), "lsm", slow=True)
        p.wait("sync", "vs")
        p.wait("sync", "lx")
        p.dma("sync", self.stat[0:1, 0:64], self.d_vec.unsqueeze(0), "lx")
        p.wait("tensor", "lx")
        p.wait("tensor", "vs")
        self.mm(self.psum[6][:, 0:64], self.ones_r128f[:, :],
                self.stat[0:1, 0:64], True, True, "tq")
        p.wait("vector", "tq")
        p.op("vector", lambda e: e.tensor_copy(
            self.scr[0:64, 2048:2112], self.psum[6][0:64, 0:64]), "vs")
        p.op("vector", lambda e: e.tensor_copy(
            self.scr[64:128, 2112:2176], self.psum[6][0:64, 0:64]), "vs")
        for half in range(2):
            for g in range(8):
                for hh in range(8):
                    h = 8 * g + hh
                    dst = HB[half * 64:half * 64 + 64,
                             16384 + g * 512 + hh * 64: 16384 + g * 512 + (hh + 1) * 64]
                    idn = self.eye128[half * 64:half * 64 + 64, half * 64:half * 64 + 64]
                    drc = self.scr[half * 64:half * 64 + 64, 2048 + half * 0 + 0:2048] if False else None
                    p.wait("vector", "vs")
                    p.op("vector", lambda e, dst=dst, h=h, idn=idn, half=half: e.tensor_scalar_mul(
                        dst, idn, (self.scr[0:64, 2048:2112] if half == 0 else
                                   self.scr[64:128, 2112:2176])[:, h:h + 1]), "vs")
        self.emit_mamba_front()
        self.emit_mamba_pass1()
        self.emit_mamba_pass2()
        self.emit_mamba_mlp()

    def emit_mamba_front(self):
        p = self.p
        F_ = self.scr
        xF = self.xln[:, :].bitcast(F32)
        p.op("vector", lambda e: e.memset(self.prs[:, 1024:1184], 0.0), "vs")
        for T in range(NTT):
            p.wait("vector", "st0")
            p.wait("scalar", "st0")
            p.wait("scalar", "st1")
            self.emit_layernorm(T)
            p.wait("sync", "vs")
            p.wait("sync", "st2")
            p.dma("sync", self.m_lnx[:, T * TT:(T + 1) * TT].rearrange("(c p) t -> p c t", p=128),
                  self.xln[:, :].rearrange("p (c t) -> p c t", t=TT), "st2")
            p.wait("vector", "st2")
            self.emit_rms_inplace(self.xln, NCH, self.g_ln, 0)
            zg = {}
            if T > 0:
                p.wait("gpsimd", "ss")
            for o in range(73):
                par = o % 2
                wi = self.wt[:, par * 2048:(par + 1) * 2048]
                lsem = "lg0" if par == 0 else "lg1"
                ow = 128 if o < 72 else 64
                if o >= 2:
                    p.wait("gpsimd", "ss", zg[o - 2])
                p.dma("gpsimd", wi[:, 0:16 * ow].rearrange("p (k o) -> p k o", o=ow),
                      self.w_in[:, o * 128: o * 128 + ow].rearrange("(k p) o -> p k o", p=128),
                      lsem)
                ps = self.psum[par][:, 0:512]
                p.wait("tensor", lsem)
                if o >= 2:
                    p.wait("tensor", "ss", zg[o - 2])
                for k in range(NCH):
                    self.mm(ps[0:ow, :], wi[:, k * ow:(k + 1) * ow],
                            self.xln[:, k * TT:(k + 1) * TT], k == 0, k == NCH - 1,
                            "ts" if k == NCH - 1 else None)
                sl = self.scrb[:, 1088 + par * 512: 1088 + (par + 1) * 512]
                stsem = "st0" if par == 0 else "st1"
                p.wait("scalar", "ts")
                if o >= 2:
                    p.wait("scalar", stsem)
                if o < 32:
                    p.op("scalar", lambda e, sl=sl, ps=ps: e.activation(sl, ps, AF.Silu), "ss")
                    zg[o] = p.cnt["ss"]
                    p.wait("sync", "ss")
                    p.dma("sync", self.m_z[o * 128:(o + 1) * 128, T * TT:(T + 1) * TT],
                          sl, stsem)
                elif o < 72:
                    p.op("scalar", lambda e, sl=sl, ps=ps: e.activation(sl, ps, AF.Copy), "ss")
                    zg[o] = p.cnt["ss"]
                    oc = o - 32
                    if T == NTT - 1:
                        hs = self.prs[:, 1024 + oc * 4: 1024 + oc * 4 + 3]
                        p.wait("vector", "ss")
                        p.op("vector", lambda e, hs=hs, sl=sl: e.tensor_copy(
                            hs, sl[:, 509:512]), "vs")
                        p.wait("sync", "vs")
                    else:
                        p.wait("sync", "ss")
                    p.dma("sync", self.m_xbc[oc * 128:(oc + 1) * 128, T * TT:(T + 1) * TT],
                          sl, stsem)
                else:
                    p.op("scalar", lambda e, ps=ps, T=T: e.activation(
                        self.scr[0:64, T * 512:(T + 1) * 512], ps[0:64, :], AF.Copy), "ss")
                    zg[o] = p.cnt["ss"]
        p.wait("vector", "ss")
        p.wait("sync", "vs")
        p.wait("sync", "st0")
        p.wait("sync", "st1")
        p.wait("sync", "st2")
        p.dma("sync", self.gb_halo_in.opt(), self.prs[:, 1024:1184], "st2")
        p.wait("gpsimd", "st2")
        self.cc_halo = self.emit_collective("AllGather", self.gb_halo_in.opt(),
                                            self.gb_halo_out.opt())
        A_ = F_[0:64, 0:1024]
        B_ = F_[64:128, 0:1024]
        C_ = F_[64:128, 1024:2048]
        E_ = F_[0:64, 2048:3072]
        G_ = F_[0:64, 1024:2048]
        dtb = self.g_nw[0:64, 32:33]
        p.wait("sync", "ss")
        p.dma("sync", self.m_dbg[:, 2048:3072], A_, "stg")
        p.wait("vector", "stg")
        p.wait("vector", "ss")
        p.op("vector", lambda e: e.tensor_scalar_add(A_, A_, dtb), "vs")
        p.wait("scalar", "vs")
        p.op("scalar", lambda e: e.activation(G_, A_, AF.Abs), "ss")
        p.wait("scalar", "ss")
        p.op("scalar", lambda e: e.activation(G_, G_, AF.Exp, scale=self.konst[0:64, 3:4]), "ss")
        p.wait("scalar", "ss")
        p.op("scalar", lambda e: e.activation(G_, G_, AF.Ln, bias=self.konst[0:64, 4:5]), "ss")
        p.wait("vector", "ss")
        p.op("vector", lambda e: e.tensor_relu(E_, A_), "vs")
        p.wait("vector", "vs")
        p.op("vector", lambda e: e.tensor_add(A_, E_, G_), "vs")
        p.wait("sync", "vs")
        p.dma("sync", self.m_dbg[:, 0:1024], A_, "stg")
        p.wait("vector", "stg")
        p.wait("scalar", "stg")
        p.wait("scalar", "vs")
        p.op("scalar", lambda e: e.activation(C_, A_, AF.Ln), "ss")
        p.op("scalar", lambda e: e.activation(self.g_nw[0:64, 34:35],
                                              self.g_nw[0:64, 33:34], AF.Exp), "ss")
        p.wait("vector", "ss")
        p.op("vector", lambda e: e.tensor_scalar_mul(self.g_nw[0:64, 34:35],
                                                     self.g_nw[0:64, 34:35], -1.0), "vs")
        p.wait("vector", "vs")
        p.op("vector", lambda e: e.tensor_scalar_mul(G_, A_, self.g_nw[0:64, 34:35]), "vs")
        p.wait("sync", "vs")
        p.dma("sync", self.m_dbg[:, 1024:2048], G_, "stg")
        p.wait("vector", "stg")
        for c in range(N_CHUNK):
            p.wait("vector", "vs")
            p.op("vector", lambda e, c=c: e.tensor_tensor_scan(
                B_[:, c * 64:(c + 1) * 64], G_[:, c * 64:(c + 1) * 64],
                G_[:, c * 64:(c + 1) * 64], 0.0, ALU.add, ALU.bypass), "vs")
        p.wait("sync", "vs")
        p.wait("sync", "st2")
        p.dma("sync", self.m_acum[:, :], B_, "st2")
        for c in range(N_CHUNK):
            last = B_[:, c * 64 + 63:c * 64 + 64]
            p.wait("vector", "vs")
            p.op("vector", lambda e, c=c, last=last: e.tensor_scalar(
                E_[:, c * 64:(c + 1) * 64], B_[:, c * 64:(c + 1) * 64], last, None,
                ALU.subtract), "vs")
        p.wait("scalar", "vs")
        p.op("scalar", lambda e: e.activation(E_, E_, AF.Exp, scale=self.konst[0:64, 3:4]), "ss")
        p.wait("vector", "ss")
        p.op("vector", lambda e: e.tensor_mul(E_, E_, A_), "vs")
        p.wait("vector", "vs")
        p.op("vector", lambda e: e.tensor_sub(G_, C_, B_), "vs")
        p.wait("sync", "vs")
        p.wait("sync", "st2")
        p.dma("sync", self.m_rows[:, 0:1024], G_, "st2")
        p.wait("scalar", "st2")
        p.wait("scalar", "vs")
        p.op("scalar", lambda e: e.activation(A_, B_, AF.Exp), "ss")
        p.wait("sync", "ss")
        p.wait("sync", "st2")
        p.dma("sync", self.m_od[:, :], A_, "st2")
        p.wait("vector", "st2")
        p.op("vector", lambda e: e.tensor_copy(G_[:, 0:16], B_[:, 63:1024:64]), "vs")
        p.wait("scalar", "vs")
        p.op("scalar", lambda e: e.activation(G_[:, 0:16], G_[:, 0:16], AF.Exp), "ss")
        p.wait("vector", "ss")
        p.op("vector", lambda e: e.tensor_tensor_scan(
            G_[:, 16:32], G_[:, 0:16], G_[:, 0:16], 1.0, ALU.mult, ALU.bypass), "vs")
        p.wait("vector", "vs")
        p.op("vector", lambda e: e.memset(G_[:, 32:33], 1.0), "vs")
        p.wait("vector", "vs")
        p.op("vector", lambda e: e.tensor_copy(G_[:, 33:48], G_[:, 16:31]), "vs")
        p.wait("sync", "vs")
        p.wait("sync", "st2")
        p.dma("sync", self.m_rows3[:, 0:16], G_[:, 0:16], "st2")
        p.wait("sync", "st2")
        p.dma("sync", self.m_rows3[:, 16:32], G_[:, 32:48], "st2")
        p.wait("vector", "st2")
        p.op("vector", lambda e: e.tensor_copy(self.g_nw[0:64, 35:36], G_[:, 31:32]), "vs")
        stg = xF[0:128, 3072:3136]
        p.wait("tensor", "vs")
        p.wait("tensor", "ss")
        for src_, dst, idn in ((C_, self.m_rows2, self.eye128[64:128, 64:128]),
                               (E_, self.m_rows2w, self.eye128[0:64, 0:64])):
            for j in range(8):
                pb = self.psum[6][:, 0:64]
                p.wait("tensor", "st2")
                p.op("tensor", lambda e, j=j, pb=pb, src_=src_, idn=idn: e.transpose(
                    pb, src_[:, j * 128:(j + 1) * 128], idn), "tp")
                p.wait("vector", "tp")
                p.op("vector", lambda e, pb=pb, stg=stg: e.tensor_copy(stg, pb), "vs")
                p.wait("sync", "vs")
                p.wait("sync", "st2")
                p.dma("sync", dst[j * 128:(j + 1) * 128, :], stg, "st2")
        p.wait("sync", "st2")
        for half in range(2):
            p.dma("sync", xF[half * 64:half * 64 + 64, 2048:3072],
                  self.m_rows2.rearrange("(c s) h -> s c h", s=64), "lx")
            p.dma("sync", xF[half * 64:half * 64 + 64, 3072:4096],
                  self.m_rows2w.rearrange("(c s) h -> s c h", s=64), "lx")
        p.wait("sync", "st1")
        for what, c0, dcol in ((0, 0, 1024), (1, 16, 0)):
            p.wait("sync", "tq")
            p.wait("sync", "lx")
            p.dma("sync", self.stat[0:1, 0:1024].rearrange("o (h c) -> o h c", c=16),
                  self.m_rows3[:, c0:c0 + 16].unsqueeze(0), "lx")
            p.wait("tensor", "lx")
            p.wait("tensor", "vs")
            for blk in range(2):
                self.mm(self.psum[6][:, 0:512], self.ones_r128f[:, :],
                        self.stat[0:1, blk * 512:(blk + 1) * 512], True, True, "tq")
                p.wait("vector", "tq")
                dst = self.scr[:, dcol + blk * 512: dcol + (blk + 1) * 512]
                p.op("vector", lambda e, dst=dst: e.tensor_copy(dst, self.psum[6][:, 0:512]), "vs")
                p.wait("tensor", "vs")
        p.wait("gpsimd", "lx")


    def emit_mamba_pass1(self):
        p = self.p
        xF = self.xln[:, :].bitcast(F32)
        prsF = self.prs[:, :].bitcast(F32)
        HB = self.hbuf
        carry = xF[:, 0:2048]
        self.mb_cg = {}
        self.mb_cv = {}
        p.wait("vector", "lx")
        p.op("vector", lambda e, carry=carry: e.memset(carry, 0.0), "vs")
        acc = self.prs[:, 1184:1344]
        p.op("vector", lambda e, acc=acc: e.memset(acc, 0.0), "vs")
        p.wait("sync", "cc", self.cc_halo)
        for j in range(NCORE):
            slot = self.prs[:, 1344:1504]
            p.wait("sync", "vs")
            p.dma("sync", slot, self.gb_halo_out[j * 128:(j + 1) * 128, :], "lx")
            p.wait("vector", "lx")
            pm = self.pmask[:, j:j + 1]
            p.wait("vector", "vs")
            p.op("vector", lambda e, slot=slot, pm=pm, acc=acc: e.scalar_tensor_tensor(
                acc, slot, pm, acc, ALU.mult, ALU.add), "vs")
        p.wait("vector", "vs")
        p.op("vector", lambda e, prsF=prsF: e.memset(prsF[64:66, 160:1184], 1.0), "vs")
        for T in range(NTT):
            for oc in range(40):
                par = oc % 2
                ci = self.scrb[:, par * 516:par * 516 + 516]
                lsem = "lw0" if par == 0 else "lw1"
                if (T, oc) != (0, 0) and (T, oc) != (0, 1):
                    p.wait("sync", "ss")
                if T == 0:
                    p.dma("sync", ci[:, 3:515],
                          self.m_xbc[oc * 128:(oc + 1) * 128, 0:512], lsem)
                    p.wait("vector", lsem)
                    p.wait("vector", "vs")
                    p.op("vector", lambda e, ci=ci, oc=oc, acc=acc: e.tensor_copy(
                        ci[:, 0:3], acc[:, oc * 4:oc * 4 + 3]), "vs")
                    p.op("vector", lambda e, ci=ci: e.memset(ci[:, 515:516], 0.0), "vs")
                else:
                    p.dma("sync", ci[:, 0:515],
                          self.m_xbc[oc * 128:(oc + 1) * 128, T * 512 - 3:(T + 1) * 512], lsem)
                    p.wait("vector", lsem)
                p.wait("vector", "ts")
                p.wait("vector", "lsm")
                for j in range(4):
                    dg = self.wt[:, 8192 + j * 128: 8192 + (j + 1) * 128]
                    cwc = self.prs[:, :].bitcast(F32)[:, oc * 4 + j: oc * 4 + j + 1]
                    p.wait("vector", "vs")
                    p.op("vector", lambda e, dg=dg, cwc=cwc: e.tensor_scalar_mul(
                        dg, self.eye128[:, :], cwc), "vs")
                ps = self.psum[4 + par][:, 0:512]
                p.wait("tensor", "vs")
                if (T, oc) not in ((0, 0), (0, 1)):
                    p.wait("tensor", "ss", self.mb_cv.get((T, oc - 2), 0))
                for j in range(4):
                    self.mm(ps, self.wt[:, 8192 + j * 128: 8192 + (j + 1) * 128],
                            ci[:, j:j + 512], j == 0, j == 3, "ts" if j == 3 else None)
                p.wait("scalar", "ts")
                stsem = "st0" if par == 0 else "st1"
                if oc < 4:
                    dst = self.wt[:, oc * 1024 + T * 512: oc * 1024 + T * 512 + 512]
                    p.op("scalar", lambda e, dst=dst, ps=ps, oc=oc: e.activation(
                        dst, ps, AF.Silu, bias=self.cbpp[:, oc:oc + 1]), "ss")
                elif oc < 8:
                    dst = self.wt[:, 4096 + (oc - 4) * 1024 + T * 512: 4096 + (oc - 4) * 1024 + T * 512 + 512]
                    p.op("scalar", lambda e, dst=dst, ps=ps, oc=oc: e.activation(
                        dst, ps, AF.Silu, bias=self.cbpp[:, oc:oc + 1]), "ss")
                else:
                    sl = self.scrb[:, 1088 + par * 512: 1088 + (par + 1) * 512]
                    p.wait("scalar", stsem)
                    p.op("scalar", lambda e, sl=sl, ps=ps, oc=oc: e.activation(
                        sl, ps, AF.Silu, bias=self.cbpp[:, oc:oc + 1]), "ss")
                    p.wait("sync", "ss")
                    p.dma("sync", self.m_cT[(oc - 8) * 128:(oc - 7) * 128, T * 512:(T + 1) * 512],
                          sl, stsem)
                self.mb_cv[(T, oc)] = p.cnt["ss"]
            p.wait("tensor", "ss")
            psb = self.psum[6][:, 0:512].bitcast(BF16)[:, 0:128]
            for which, dst0 in ((0, 20480), (1, 22528)):
                for fc in range(4):
                    for tt in range(4):
                        srcap = self.wt[:, which * 4096 + fc * 1024 + T * 512 + tt * 128:
                                        which * 4096 + fc * 1024 + T * 512 + (tt + 1) * 128]
                        p.wait("tensor", "vs")
                        p.op("tensor", lambda e, srcap=srcap, psb=psb: e.transpose(
                            psb, srcap, self.eye128b[:, :]), "tp")
                        p.wait("vector", "tp")
                        dst = HB[:, dst0 + tt * 512 + fc * 128: dst0 + tt * 512 + (fc + 1) * 128]
                        p.op("vector", lambda e, dst=dst, psb=psb: e.tensor_copy(dst, psb), "vs")
            p.wait("sync", "st0")
            p.wait("sync", "st1")
            if T > 0:
                p.wait("sync", "ts")
            for plane in range(2):
                for gg in range(4):
                    p.dma("sync", HB[plane * 64:(plane + 1) * 64, gg * 4096:(gg + 1) * 4096]
                          .rearrange("p (h l) -> p h l", h=8),
                          self.m_cT[:, T * 512:(T + 1) * 512]
                          .rearrange("(g q h n) l -> q g n h l", g=4, q=2, h=8)[plane, gg],
                          "lw2")
            p.wait("tensor", "lw2")
            hbF = HB[:, :].bitcast(F32)
            p.wait("tensor", "vs")
            p.wait("tensor", "ss")
            if T == 0:
                p.wait("vector", "vs")
                p.op("vector", lambda e, hbF=hbF: e.memset(hbF[64:66, 12288:14336], 1.0), "vs")
                self.k2g = {}
                self.dveg = {}
                self.ystg = {}
            for c8 in range(8):
                c = T * 8 + c8
                cb = (c8 % 2) * 64
                p.wait("sync", "vs")
                p.dma("sync", self.m_prev[c * 128:(c + 1) * 128, :], carry, "st1")
                for g in range(8):
                    idx = c * 8 + g
                    par = idx % 2
                    plane = g % 2
                    gg = g // 2
                    pb_ = plane * 64
                    lhsK = hbF[64:66, 12288 + par * 1024: 12288 + par * 1024 + 512]
                    rhsK = hbF[64:66, 12288 + par * 1024 + 512: 12288 + par * 1024 + 1024]
                    if idx - 2 in self.k2g:
                        p.wait("sync", "tq", self.k2g[idx - 2])
                    p.dma("sync", lhsK[1:2, :].rearrange("o (h s) -> o h s", h=8),
                          self.m_rows[:, :].rearrange("(hh h) t -> hh h t", hh=8)
                          [g:g + 1, :, c * 64:(c + 1) * 64].rearrange("a h t -> a h t"),
                          "lx")
                    p.dma("sync", rhsK[0:1, :].rearrange("o (h s) -> o h s", h=8),
                          self.m_acum[:, :].rearrange("(hh h) t -> hh h t", hh=8)
                          [g:g + 1, :, c * 64:(c + 1) * 64], "lx")
                    k2lx = p.cnt["lx"]
                    bch = g // 2
                    bbase = (g % 2) * 64
                    lhsB = self.wt[bbase:bbase + 64,
                                   4096 + bch * 1024 + T * 512 + c8 * 64:
                                   4096 + bch * 1024 + T * 512 + c8 * 64 + 64]
                    rhsC = HB[pb_:pb_ + 64, gg * 4096: (gg + 1) * 4096] \
                        .rearrange("n (h l) -> n h l", h=8)[:, :, c8 * 64:(c8 + 1) * 64]
                    psT0 = self.psum[0 if par == 0 else 5][0:64, 0:512]
                    psSS = self.psum[1 if par == 0 else 6][0:64, 0:512]
                    p.wait("tensor", "lx", k2lx)
                    if idx - 2 in self.dveg:
                        p.wait("tensor", "vs", self.dveg[idx - 2])
                    self.mm(psT0, lhsB, rhsC, True, True, "ts")
                    for hh in range(8):
                        self.mm(psSS[0:64, hh * 64:(hh + 1) * 64],
                                lhsK[:, hh * 64:(hh + 1) * 64],
                                rhsK[:, hh * 64:(hh + 1) * 64],
                                True, True, "ts" if hh == 7 else None)
                    tsv = p.cnt["ts"]
                    EMf = self.scr[cb:cb + 64, 2112:2624]
                    EMb = self.scrb[cb:cb + 64, 0:512]
                    Tpp = self.scrb[cb:cb + 64, 512:1024]
                    p.wait("vector", "ts", tsv)
                    for hh in range(8):
                        h = 8 * g + hh
                        cap = xF[0:64, 2048 + c * 64 + h: 2048 + c * 64 + h + 1]
                        p.wait("vector", "vs")
                        p.op("vector", lambda e, EMf=EMf, hh=hh, cap=cap, psSS=psSS: e.tensor_scalar_min(
                            EMf[:, hh * 64:(hh + 1) * 64],
                            psSS[0:64, hh * 64:(hh + 1) * 64], cap), "vs")
                    p.wait("scalar", "vs")
                    p.op("scalar", lambda e, EMb=EMb, EMf=EMf: e.activation(EMb, EMf, AF.Exp), "ss")
                    p.wait("vector", "ss")
                    trl = self.tril64[cb:cb + 64, :]
                    p.op("vector", lambda e, EMb=EMb, trl=trl: e.tensor_mul(
                        EMb.rearrange("n (h l) -> n h l", h=8),
                        EMb.rearrange("n (h l) -> n h l", h=8),
                        trl.unsqueeze(1).broadcast_to([64, 8, 64])), "vs")
                    p.wait("vector", "vs")
                    p.op("vector", lambda e, Tpp=Tpp, EMb=EMb, psT0=psT0: e.tensor_mul(
                        Tpp, psT0, EMb), "vs")
                    p.wait("vector", "vs")
                    dimg = HB[cb:cb + 64, 16384 + g * 512:16384 + (g + 1) * 512]
                    p.op("vector", lambda e, Tpp=Tpp, dimg=dimg: e.tensor_add(Tpp, Tpp, dimg), "vs")
                    tppv = p.cnt["vs"]
                    xts = HB[cb:cb + 64, 20480 + (c8 // 2) * 512 + g * 64:
                             20480 + (c8 // 2) * 512 + (g + 1) * 64]
                    p.wait("tensor", "vs", tppv)
                    if idx - 1 in self.ystg:
                        p.wait("tensor", "vs", max(self.ystg[idx - 1], tppv))
                    self.mm(self.psum[2][0:64, 0:512], xts, Tpp, True, True)
                    Xw = self.scrb[cb:cb + 64, 1088:1600]
                    for hh in range(8):
                        h = 8 * g + hh
                        w = xF[0:64, 3072 + c * 64 + h: 3072 + c * 64 + h + 1]
                        p.wait("vector", "vs")
                        p.op("vector", lambda e, Xw=Xw, hh=hh, xts=xts, w=w: e.tensor_scalar_mul(
                            Xw[:, hh * 64:(hh + 1) * 64], xts, w), "vs")
                    xwv = p.cnt["vs"]
                    self.dveg[idx] = xwv
                    btx = HB[cb:cb + 64, 22528 + (c8 // 2) * 512 + g * 64:
                             22528 + (c8 // 2) * 512 + (g + 1) * 64]
                    p.wait("tensor", "vs", xwv)
                    self.mm(self.psum[3][0:64, 0:512], btx, Xw, True, True, "tq")
                    stv = p.cnt["tq"]
                    self.k2g[idx] = stv
                    yst = self.scrb[cb:cb + 64, 1600:2112]
                    p.wait("vector", "tq", stv)
                    p.wait("vector", "st0")
                    p.op("vector", lambda e, yst=yst: e.tensor_copy(
                        yst, self.psum[2][0:64, 0:512]), "vs")
                    self.ystg[idx] = p.cnt["vs"]
                    p.wait("sync", "vs")
                    dstY = self.m_yd2[pb_:pb_ + 64, T * 16384 + gg * 4096: T * 16384 + (gg + 1) * 4096] \
                        .rearrange("n (h l) -> n h l", h=8)[:, :, c8 * 64:(c8 + 1) * 64]
                    p.dma("sync", dstY, yst.rearrange("n (h l) -> n h l", h=8), "st0")
                    cg = carry[pb_:pb_ + 64, gg * 512:(gg + 1) * 512]
                    p.wait("vector", "st1")
                    for hh in range(8):
                        h = 8 * g + hh
                        dcol = self.scr[pb_:pb_ + 64, 1024 + h * 16 + c: 1024 + h * 16 + c + 1]
                        p.wait("vector", "vs")
                        p.op("vector", lambda e, cg=cg, hh=hh, dcol=dcol: e.tensor_scalar_mul(
                            cg[:, hh * 64:(hh + 1) * 64], cg[:, hh * 64:(hh + 1) * 64], dcol), "vs")
                    p.wait("vector", "vs")
                    p.op("vector", lambda e, cg=cg: e.tensor_add(
                        cg, cg, self.psum[3][0:64, 0:512]), "vs")
        p.wait("sync", "vs")
        p.wait("sync", "st0")
        p.wait("sync", "st1")
        p.wait("sync", "st2")
        p.dma("sync", self.gb_seg_in[:, 0:2048], carry, "st2")
        p.wait("sync", "st2")
        p.dma("sync", self.m_rows3[:, 16:17], self.g_nw[0:64, 35:36], "st2", slow=True)
        p.wait("sync", "st2")
        p.dma("sync", self.stat[0:1, 1024:1088],
              self.m_rows3[:, 16:17].rearrange("h c -> c h"), "lx", slow=True)
        onesq = self.scrb[0:64, 0:64]
        dstage = self.xln[:, :].bitcast(F32)[0:64, 3584:4096]
        dcol = self.xln[:, :].bitcast(F32)[0:64, 3072:3073]
        p.wait("vector", "st2")
        p.op("vector", lambda e, onesq=onesq: e.memset(onesq, 1.0), "vs")
        p.wait("tensor", "lx")
        for plane in range(2):
            for gg in range(4):
                for hh in range(8):
                    h = 16 * gg + 8 * plane + hh
                    p.wait("tensor", "vs")
                    self.mm(self.psum[6][0:64, 0:1], self.ones_r128f[0:1, 0:64],
                            self.stat[0:1, 1024 + h:1025 + h], True, True, "tq")
                    p.wait("vector", "tq")
                    p.op("vector", lambda e, dcol=dcol: e.tensor_copy(
                        dcol, self.psum[6][0:64, 0:1]), "vs")
                    p.wait("vector", "vs")
                    p.op("vector", lambda e, hh=hh, dcol=dcol, dstage=dstage, onesq=onesq:
                         e.tensor_scalar_mul(dstage[:, hh * 64:(hh + 1) * 64], onesq, dcol), "vs")
                p.wait("sync", "vs")
                p.wait("sync", "st2")
                p.dma("sync", self.gb_seg_in[plane * 64:(plane + 1) * 64,
                                             2048 + gg * 512: 2048 + (gg + 1) * 512],
                      dstage, "st2")
                p.wait("vector", "st2")
        p.wait("gpsimd", "st0")
        p.wait("gpsimd", "st1")
        p.wait("gpsimd", "st2")
        self.cc_seg = self.emit_collective("AllGather", self.gb_seg_in.opt(),
                                           self.gb_seg_out.opt())

    def emit_mamba_pass2(self):
        p = self.p
        xF = self.xln[:, :].bitcast(F32)
        HB = self.hbuf
        acc = xF[:, 0:2048]
        Pbuf = xF[:, 2048:4096]
        p.wait("sync", "cc", self.cc_seg)
        p.wait("vector", "vs")
        p.op("vector", lambda e, acc=acc: e.memset(acc, 0.0), "vs")
        for j in range(NCORE):
            p.wait("sync", "vs")
            p.dma("sync", Pbuf, self.gb_seg_out[j * 128:(j + 1) * 128, 2048:4096], "lx")
            p.wait("vector", "lx")
            rm = self.rmask[:, j:j + 1]
            ri = self.rinv[:, j:j + 1]
            p.wait("vector", "vs")
            p.op("vector", lambda e, Pbuf=Pbuf, rm=rm, ri=ri: e.tensor_scalar(
                Pbuf, Pbuf, rm, ri, ALU.mult, ALU.add), "vs")
            p.wait("vector", "vs")
            p.op("vector", lambda e, acc=acc, Pbuf=Pbuf: e.tensor_mul(acc, acc, Pbuf), "vs")
            p.wait("sync", "vs")
            p.dma("sync", Pbuf, self.gb_seg_out[j * 128:(j + 1) * 128, 0:2048], "lx")
            p.wait("vector", "lx")
            p.op("vector", lambda e, acc=acc, Pbuf=Pbuf, rm=rm: e.scalar_tensor_tensor(
                acc, Pbuf, rm, acc, ALU.mult, ALU.add), "vs")
        self.p2g = {}
        self.p2s = {}
        for T in range(NTT):
            p.wait("sync", "ts")
            for plane in range(2):
                for gg in range(4):
                    p.dma("sync", HB[plane * 64:(plane + 1) * 64, gg * 4096:(gg + 1) * 4096]
                          .rearrange("p (h l) -> p h l", h=8),
                          self.m_cT[:, T * 512:(T + 1) * 512]
                          .rearrange("(g q h n) l -> q g n h l", g=4, q=2, h=8)[plane, gg],
                          "lw2")
            p.wait("gpsimd", "st0")
            p.wait("gpsimd", "st1")
            p.wait("vector", "lw2")
            for blk in range(32):
                par = blk % 2
                gg, hh8 = blk // 8, blk % 8
                odb = self.scrb[:, 1088 + par * 512: 1088 + (par + 1) * 512]
                for plane in range(2):
                    h = 16 * gg + 8 * plane + hh8
                    srcb = self.m_od[h:h + 1, T * 512:(T + 1) * 512].broadcast_to([64, 512])
                    p.wait("gpsimd", "vs")
                    p.dma("gpsimd", odb[plane * 64:plane * 64 + 64, :], srcb, "lgs")
                p.wait("vector", "lgs")
                dst = HB[:, blk * 512:(blk + 1) * 512]
                p.wait("vector", "vs")
                p.op("vector", lambda e, dst=dst, odb=odb: e.tensor_mul(dst, dst, odb), "vs")
            for c8 in range(8):
                c = T * 8 + c8
                p.wait("sync", "vs")
                p.wait("sync", "tq")
                p.dma("sync", Pbuf, self.m_prev[c * 128:(c + 1) * 128, :], "lx")
                p.wait("vector", "lx")
                for g in range(8):
                    plane, gg = g % 2, g // 2
                    pb_ = plane * 64
                    for hh in range(8):
                        h = 8 * g + hh
                        pe = self.scr[pb_:pb_ + 64, h * 16 + c: h * 16 + c + 1]
                        sl = (slice(pb_, pb_ + 64), slice(gg * 512 + hh * 64, gg * 512 + (hh + 1) * 64))
                        p.wait("vector", "vs")
                        p.op("vector", lambda e, sl=sl, pe=pe: e.scalar_tensor_tensor(
                            Pbuf[sl[0], sl[1]], acc[sl[0], sl[1]], pe,
                            Pbuf[sl[0], sl[1]], ALU.mult, ALU.add), "vs")
                Pb16 = HB[:, 16384:18432]
                p.wait("vector", "vs")
                p.op("vector", lambda e, Pb16=Pb16, Pbuf=Pbuf: e.tensor_copy(Pb16, Pbuf), "vs")
                pv = p.cnt["vs"]
                for g in range(8):
                    idx2 = c8 * 8 + g + T * 64
                    par2 = idx2 % 2
                    plane, gg = g % 2, g // 2
                    pb_ = plane * 64
                    psY = self.psum[2 if par2 == 0 else 5][0:64, 0:512]
                    p.wait("tensor", "vs", pv)
                    if idx2 - 2 in self.p2g:
                        p.wait("tensor", "vs", max(pv, self.p2g[idx2 - 2]))
                    for hh in range(8):
                        lhsP = Pb16[pb_:pb_ + 64, gg * 512 + hh * 64: gg * 512 + (hh + 1) * 64]
                        rhsC = HB[pb_:pb_ + 64, (gg * 8 + hh) * 512 + c8 * 64:
                                  (gg * 8 + hh) * 512 + c8 * 64 + 64]
                        self.mm(psY[0:64, hh * 64:(hh + 1) * 64], lhsP, rhsC,
                                True, True, "tq" if hh == 7 else None)
                    tqv = p.cnt["tq"]
                    yd = self.scrb[par2 * 64:par2 * 64 + 64, 1600:2112]
                    srcY = self.m_yd2[pb_:pb_ + 64, T * 16384 + gg * 4096: T * 16384 + (gg + 1) * 4096] \
                        .rearrange("n (h l) -> n h l", h=8)[:, :, c8 * 64:(c8 + 1) * 64]
                    stsem2 = "st0" if par2 == 0 else "st1"
                    if idx2 - 2 in self.p2s:
                        p.wait("sync", stsem2, self.p2s[idx2 - 2])
                    p.dma("sync", yd.rearrange("n (h l) -> n h l", h=8), srcY, "lw3")
                    ydl = p.cnt["lw3"]
                    p.wait("vector", "lw3", ydl)
                    p.wait("vector", "tq", tqv)
                    p.op("vector", lambda e, yd=yd, psY=psY: e.tensor_add(
                        yd, yd, psY), "vs")
                    self.p2g[idx2] = p.cnt["vs"]
                    p.wait("sync", "vs")
                    p.dma("sync", srcY, yd.rearrange("n (h l) -> n h l", h=8), stsem2)
                    self.p2s[idx2] = p.cnt[stsem2]
            p.wait("sync", "st0")
            p.wait("sync", "st1")
            for fc in range(32):
                par = fc % 2
                r = fc % 8
                q, gg = r // 4, fc // 8
                ysl = HB[:, 16384 + fc * 512: 16384 + (fc + 1) * 512]
                zsl = self.scrb[:, par * 512:(par + 1) * 512]
                base = T * 16384 + gg * 4096 + (2 * (r % 4)) * 512
                p.wait("sync", "vs")
                p.dma("sync", ysl[0:64, :],
                      self.m_yd2[64 * q: 64 * q + 64, base: base + 512], "lw3")
                p.dma("sync", ysl[64:128, :],
                      self.m_yd2[64 * q: 64 * q + 64, base + 512: base + 1024], "lw3")
                p.dma("sync", zsl, self.m_z[fc * 128:(fc + 1) * 128, T * 512:(T + 1) * 512], "lw0")
                p.wait("vector", "lw3")
                p.wait("vector", "lw0")
                p.op("vector", lambda e, ysl=ysl, zsl=zsl: e.tensor_mul(ysl, ysl, zsl), "vs")
            ubuf = HB[:, 16384:32768]
            self.emit_rms_inplace(ubuf, 32, self.g_nw, 0)
            wg = {}
            p.wait("gpsimd", "lg0")
            p.wait("gpsimd", "lg1")
            p.wait("tensor", "vs")
            for f in range(NCH):
                par = f % 2
                wi = self.wt[:, par * 4096:(par + 1) * 4096]
                lsem = "lg0" if par == 0 else "lg1"
                if f >= 2:
                    p.wait("gpsimd", "vs", wg[f - 2])
                p.dma("gpsimd", wi[:, 0:4096].rearrange("p (k o) -> p k o", o=128),
                      self.w_out[:, f * 128:(f + 1) * 128].rearrange("(k p) o -> p k o", p=128),
                      lsem)
                ps = self.psum[4 + par][:, 0:512]
                p.wait("tensor", lsem)
                if f >= 2:
                    p.wait("tensor", "vs", wg[f - 2])
                for k in range(32):
                    self.mm(ps, wi[:, k * 128:(k + 1) * 128],
                            ubuf[:, k * 512:(k + 1) * 512], k == 0, k == 31,
                            "ts" if k == 31 else None)
                lnx = self.scrb[:, par * 512:(par + 1) * 512]
                p.wait("sync", "ts")
                p.dma("sync", lnx, self.m_lnx[f * 128:(f + 1) * 128, T * 512:(T + 1) * 512], "lw0")
                p.wait("vector", "lw0")
                p.wait("vector", "ts")
                inn = self.scrb[:, 1088 + par * 512: 1088 + (par + 1) * 512]
                p.wait("vector", "st1")
                p.op("vector", lambda e, inn=inn, ps=ps, lnx=lnx: e.tensor_add(inn, ps, lnx), "vs")
                wg[f] = p.cnt["vs"]
                p.wait("sync", "vs")
                p.dma("sync", self.m_mo[f * 128:(f + 1) * 128, T * 512:(T + 1) * 512], inn, "st1")

    def emit_mamba_mlp(self):
        p = self.p
        HB = self.hbuf
        p.wait("gpsimd", "ts")
        p.wait("gpsimd", "tq")
        for T in range(NTT):
            p.wait("sync", "st1")
            p.wait("sync", "ts")
            p.wait("sync", "vs")
            p.dma("sync", self.xln[:, :].rearrange("p (c t) -> p c t", t=TT),
                  self.m_mo[:, T * TT:(T + 1) * TT].rearrange("(c p) t -> p c t", p=128), "lw0")
            p.wait("vector", "lw0")
            p.wait("vector", "ss")
            self.emit_rms_inplace(self.xln, NCH, self.g_ln, NCH)
            mg = {}
            p.wait("gpsimd", "lg0")
            p.wait("gpsimd", "lg1")
            for i in range(44):
                par = i % 2
                wgt = self.wt[:, par * 2048:(par + 1) * 2048]
                wut = self.wt[:, 4096 + par * 2048: 4096 + (par + 1) * 2048]
                lg = "lg0" if par == 0 else "lg1"
                if i >= 2:
                    p.wait("gpsimd", "vs", mg[i - 2])
                p.dma("gpsimd", wgt.rearrange("p (k o) -> p k o", o=128),
                      self.mgw[:, i * 128:(i + 1) * 128].rearrange("(k p) o -> p k o", p=128), lg)
                p.dma("gpsimd", wut.rearrange("p (k o) -> p k o", o=128),
                      self.muw[:, i * 128:(i + 1) * 128].rearrange("(k p) o -> p k o", p=128), lg)
                ps_g = self.psum[par][:, 0:512]
                ps_u = self.psum[2 + par][:, 0:512]
                p.wait("tensor", lg)
                if i >= 2:
                    p.wait("tensor", "vs", mg[i - 2])
                    p.wait("tensor", "ss")
                for k in range(NCH):
                    self.mm(ps_g, wgt[:, k * 128:(k + 1) * 128],
                            self.xln[:, k * TT:(k + 1) * TT], k == 0, k == NCH - 1)
                for k in range(NCH):
                    self.mm(ps_u, wut[:, k * 128:(k + 1) * 128],
                            self.xln[:, k * TT:(k + 1) * TT], k == 0, k == NCH - 1,
                            "ts" if k == NCH - 1 else None)
                sg = self.scrb[:, par * 512:(par + 1) * 512]
                ht = HB[:, i * 512:(i + 1) * 512]
                p.wait("scalar", "ts")
                p.op("scalar", lambda e, sg=sg, ps_g=ps_g: e.activation(sg, ps_g, AF.Silu), "ss")
                p.wait("vector", "ss")
                p.op("vector", lambda e, ht=ht, ps_u=ps_u, sg=sg: e.tensor_mul(ht, ps_u, sg), "vs")
                mg[i] = p.cnt["vs"]
            p.wait("tensor", "vs")
            for fg in range(4):
                tg = {}
                p.wait("gpsimd", "ts")
                p.wait("gpsimd", "lg0")
                p.wait("gpsimd", "lg1")
                for i in range(44):
                    par = i % 2
                    wd = self.wt[:, 8192 + par * 512: 8192 + (par + 1) * 512]
                    lg = "lg0" if par == 0 else "lg1"
                    if i >= 2:
                        p.wait("gpsimd", "ts", tg[i - 2])
                    p.dma("gpsimd", wd,
                          self.mdw[i * 128:(i + 1) * 128, fg * 512:(fg + 1) * 512], lg)
                    p.wait("tensor", lg)
                    if i == 0:
                        p.wait("tensor", "vs")
                    for j in range(4):
                        self.mm(self.psum[j][:, 0:512],
                                wd[:, j * 128:(j + 1) * 128],
                                HB[:, i * 512:(i + 1) * 512],
                                i == 0, i == 43, "ts" if j == 3 else None)
                    tg[i] = p.cnt["ts"]
                p.wait("vector", "ts")
                for j in range(4):
                    f = fg * 4 + j
                    par = j % 2
                    inn = self.scrb[:, par * 512:(par + 1) * 512]
                    p.wait("sync", "vs")
                    p.dma("sync", inn,
                          self.m_mo[f * 128:(f + 1) * 128, T * TT:(T + 1) * TT], "lw0")
                    p.wait("vector", "lw0")
                    tmp = self.scrb[:, 1088 + par * 512: 1088 + (par + 1) * 512]
                    p.wait("vector", "vs")
                    p.op("vector", lambda e, tmp=tmp, j=j, inn=inn: e.tensor_add(
                        tmp, self.psum[j][:, 0:512], inn), "vs")
                    dst = self.xres[:, f * TOK + T * TT: f * TOK + T * TT + TT]
                    p.wait("vector", "vs")
                    p.op("vector", lambda e, dst=dst, tmp=tmp: e.tensor_add(dst, dst, tmp), "vs")
                p.wait("tensor", "vs")

    def finalize(self):
        block = self.es.enter_context(self.nc.Block())
        self.p.finalize(block)
        self.es.__exit__(None, None, None)
        return self.nc


def build_nc(parts=("all",)):
    mk = MK()
    mk.emit_smalls()
    mk.emit_input()
    P_ = mk.p
    full = "all" in parts

    def prep_ffn0():
        mk.emit_prep(mk.g0w, mk.st_g0, mk.g0T, 8192, D)
        mk.emit_prep(mk.u0w, mk.st_u0, mk.u0T, 8192, D)
        mk.emit_prep(mk.d0w, mk.st_d0, mk.d0T, D, 8192)
        P_.wait("sync", "sp0")
        P_.wait("sync", "sp1")

    def prep_attn():
        mk.at_tq = {}
        mk.at_vss = {}
        mk.at_ss_gate = {}
        cds = [mk.emit_cast(mk.wq_h, mk.st_qkv[0:256, :], 256, D),
               mk.emit_cast(mk.wk_h, mk.st_qkv[256:512, :], 256, D),
               mk.emit_cast(mk.wv_h, mk.st_qkv[512:768, :], 256, D)]
        mk.emit_transpose_prep(mk.st_qkv, mk.qkvT, 768, D, cds)
        mk.emit_prep(mk.wo, mk.st_wo, mk.woT, D, D)
        P_.wait("sync", "sp0")
        P_.wait("sync", "sp1")
        P_.wait("sync", "ts")

    def prep_ffn1():
        mk.emit_prep(mk.f1w0, mk.st_f0, mk.f1w0T, 8192, D)
        mk.emit_prep(mk.f1w1, mk.st_f1, mk.f1w1T, D, 8192)
        P_.wait("sync", "sp0")
        P_.wait("sync", "sp1")
        P_.wait("sync", "ts")

    if full:
        prep_ffn0()
        mk.emit_ffn_quant()
        prep_attn()
        mk.emit_attn()
        P_.wait("sync", "ts")
        P_.wait("sync", "tq")
        mk.emit_ffn_quant()
        prep_ffn1()
        mk.emit_ffn_gelu()
        mk.emit_mamba()
        P_.wait("sync", "ts")
        P_.wait("sync", "tq")
        mk.emit_ffn_gelu()
    else:
        if "ffn0" in parts:
            prep_ffn0()
            mk.emit_ffn_quant()
        if "attn" in parts:
            prep_attn()
            mk.emit_attn()
        if "ffn1" in parts:
            prep_ffn1()
            mk.emit_ffn_gelu()
        if "mamba" in parts:
            mk.emit_mamba()
    mk.emit_output()
    return mk.finalize()


def make_inmaps(inputs, parts=("all",)):
    bf = ml_dtypes.bfloat16
    x = np.asarray(inputs["L0_attn_x"]).reshape(SEQ, D)
    Wq = np.asarray(inputs["L0_attn_Wq"])
    Wk = np.asarray(inputs["L0_attn_Wk"])
    Wv = np.asarray(inputs["L0_attn_Wv"])
    in_maps = []
    for core in range(NCORE):
        m = {
            "xin": np.ascontiguousarray(x[core * TOK:(core + 1) * TOK]),
            "wq_h": np.ascontiguousarray(Wq[core * 256:(core + 1) * 256]),
            "wk_h": np.ascontiguousarray(Wk[core * 256:(core + 1) * 256]),
            "wv_h": np.ascontiguousarray(Wv[core * 256:(core + 1) * 256]),
            "wo": np.asarray(inputs["L0_attn_Wo"]),
            "g0w": np.asarray(inputs["L0_ffn_gate_w"], dtype=np.int32),
            "u0w": np.asarray(inputs["L0_ffn_up_w"], dtype=np.int32),
            "d0w": np.asarray(inputs["L0_ffn_down_w"], dtype=np.int32),
            "g0s": np.asarray(inputs["L0_ffn_gate_scale"]),
            "u0s": np.asarray(inputs["L0_ffn_up_scale"]),
            "d0s": np.asarray(inputs["L0_ffn_down_scale"]),
            "w_in": np.asarray(inputs["L1_attn_w_in"]),
            "conv_w": np.asarray(inputs["L1_attn_conv_w"]),
            "conv_b": np.asarray(inputs["L1_attn_conv_b"]),
            "dt_bias": np.asarray(inputs["L1_attn_dt_bias"]),
            "a_log": np.asarray(inputs["L1_attn_A_log"]),
            "d_vec": np.asarray(inputs["L1_attn_D"]),
            "norm_w": np.asarray(inputs["L1_attn_norm_w"]),
            "w_out": np.asarray(inputs["L1_attn_w_out"]),
            "ln1_w": np.asarray(inputs["L1_attn_ln1_w"]),
            "ln2_w": np.asarray(inputs["L1_attn_ln2_w"]),
            "mgw": np.asarray(inputs["L1_attn_gate_w"]),
            "muw": np.asarray(inputs["L1_attn_up_w"]),
            "mdw": np.asarray(inputs["L1_attn_down_w"]),
            "f1w0": np.asarray(inputs["L1_ffn_W0"]),
            "f1b0": np.asarray(inputs["L1_ffn_bias0"]),
            "f1w1": np.asarray(inputs["L1_ffn_W1"]),
            "f1b1": np.asarray(inputs["L1_ffn_bias1"]),
        }
        m.update(host_constants(core))
        in_maps.append(m)
    return in_maps


def kernel(**inputs):
    nc = build_nc(("all",))
    in_maps = make_inmaps(inputs)
    res = run_bass_kernel_spmd(nc, in_maps, core_ids=list(range(NCORE)))
    y = np.concatenate([res.results[i]["yout"] for i in range(NCORE)], axis=0)
    return y.reshape(1, SEQ, D).astype(np.float32)
